# version 22
# baseline (speedup 1.0000x reference)
import sys
import numpy as np

try:
    import concourse.bass as bass
except ImportError:
    sys.path.insert(0, "/opt/trn_rl_repo")
    import concourse.bass as bass
import concourse.mybir as mybir
from concourse.bass_utils import run_bass_kernel_spmd
from concourse.bass_core_rust import IndirectOffsetOnAxis

F32 = mybir.dt.float32
I32 = mybir.dt.int32
BF16 = mybir.dt.bfloat16
AF = mybir.ActivationFunctionType
ALU = mybir.AluOpType
AX = mybir.AxisListType

N_CORES = 8
D = 1024
KT = D // 128
E = 8
DECAY = 0.9
INPUT_SCALE = 0.1
EPS = 1e-5


class _Op:
    __slots__ = ("fn", "eng", "sem", "needs_inc", "inc_amt", "token", "waits")

    def __init__(self, fn, eng, sem, dma):
        self.fn = fn
        self.eng = eng
        self.sem = sem
        self.needs_inc = bool(dma)
        self.inc_amt = 16 if dma else 1
        self.token = None
        self.waits = []


_SELF_SYNC = {"pe": False, "dve": True, "act": True, "qs": True, "qg": True}


class Sched:

    def __init__(self, nc):
        self.nc = nc
        self.ops = {k: [] for k in _SELF_SYNC}
        self.sems = {k: nc.alloc_semaphore("sem_" + k) for k in _SELF_SYNC}
        self.last_writer = {}
        self.readers = {}
        self.cowriters = {}
        self.fillers = []
        self.grouped_sems = {"d_gsc", "d_idg", "d_xsp0", "d_xsp1",
                             "d_x1sd0", "d_x1sd1", "d_ini_g", "d_ini_s"}
        self.sem_last = {}

    def pump(self, n=1):
        for _ in range(n):
            if not self.fillers:
                return
            self.fillers.pop(0)()

    def drain(self):
        while self.fillers:
            self.fillers.pop(0)()

    def _sem(self, name):
        if name not in self.sems:
            self.sems[name] = self.nc.alloc_semaphore("sem_" + name)
        return self.sems[name]

    def emit(self, eng, fn, reads=(), writes=(), dma=False, dsem=None,
             no_waw=False):
        sem = eng
        if dma:
            sem = "d_" + (dsem if dsem is not None else writes[0])
            self._sem(sem)
        op = _Op(fn, eng, sem, dma)
        deps = []
        for k in reads:
            w = self.last_writer.get(k)
            if w is not None:
                deps.append(w)
            deps.extend(self.cowriters.get(k, ()))
        for k in writes:
            w = self.last_writer.get(k)
            if w is not None:
                deps.append(w)
            deps.extend(self.readers.get(k, ()))
            if not no_waw:
                deps.extend(self.cowriters.get(k, ()))
        for pop in deps:
            if pop is op:
                continue
            if pop.sem in self.grouped_sems:
                pop = self.sem_last[pop.sem]
            if pop.eng == eng and pop.sem == sem and not _SELF_SYNC[eng]:
                continue
            if pop not in op.waits:
                pop.needs_inc = True
                op.waits.append(pop)
        for k in reads:
            self.readers.setdefault(k, []).append(op)
        for k in writes:
            if no_waw:
                self.cowriters.setdefault(k, []).append(op)
            else:
                self.last_writer[k] = op
                self.readers[k] = []
                self.cowriters[k] = []
        if op.inc_amt == 16:
            self.sem_last[sem] = op
        self.ops[eng].append(op)
        return op

    def defer(self, eng, fn, reads=(), writes=(), dma=False, dsem=None):
        self.fillers.append(
            lambda eng=eng, fn=fn, r=tuple(reads), w=tuple(writes), d=dma,
            ds=dsem: self.emit(eng, fn, r, w, dma=d, dsem=ds))

    def finalize_tokens(self):
        counts = {}
        for eng, ops in self.ops.items():
            for op in ops:
                if op.needs_inc:
                    c = counts.get(op.sem, 0) + op.inc_amt
                    counts[op.sem] = c
                    op.token = c
        self.totals = counts

    def replay(self, eng, e):
        watermark = {}
        for op in self.ops[eng]:
            want = {}
            for pop in op.waits:
                assert pop.token is not None
                if pop.token > want.get(pop.sem, -1):
                    want[pop.sem] = pop.token
            for s, v in sorted(want.items()):
                if v > watermark.get(s, 0):
                    e.wait_ge(self.sems[s], v)
                    watermark[s] = v
            instr = op.fn(e)
            if op.needs_inc:
                instr.then_inc(self.sems[op.sem], op.inc_amt)


def build_program(SEQ=4096, CHUNK=1024, H=4096):
    TPC = CHUNK // 128
    NCH = SEQ // CHUNK
    HC = H // 128
    CAP = CHUNK // 2
    GT = CAP // 128

    nc = bass.Bass("TRN2", target_bir_lowering=False)

    d_x = nc.dram_tensor("x", [SEQ, D], F32, kind="ExternalInput").ap()
    d_win = nc.dram_tensor("Win", [D, 2 * D], F32, kind="ExternalInput").ap()
    d_bin = nc.dram_tensor("bin", [1, 2 * D], F32, kind="ExternalInput").ap()
    d_wout = nc.dram_tensor("Wout", [D, D], F32, kind="ExternalInput").ap()
    d_bout = nc.dram_tensor("bout", [1, D], F32, kind="ExternalInput").ap()
    d_gw = nc.dram_tensor("gw", [D, E], F32, kind="ExternalInput").ap()
    d_gb = nc.dram_tensor("gb", [1, E], F32, kind="ExternalInput").ap()
    d_w1 = nc.dram_tensor("w1", [E, D, H], F32, kind="ExternalInput").ap()
    d_b1 = nc.dram_tensor("b1", [E * HC, 128], F32, kind="ExternalInput").ap()
    d_w2 = nc.dram_tensor("w2", [E, H, D], F32, kind="ExternalInput").ap()
    d_b2 = nc.dram_tensor("b2", [E, D], F32, kind="ExternalInput").ap()
    d_at = nc.dram_tensor("cAT", [128, 128], F32, kind="ExternalInput").ap()
    d_dec = nc.dram_tensor("cdec", [1, 128], F32, kind="ExternalInput").ap()
    d_c128 = nc.dram_tensor("c128", [1, 1], F32, kind="ExternalInput").ap()
    d_id = nc.dram_tensor("cid", [128, 128], F32, kind="ExternalInput").ap()
    d_ltu = nc.dram_tensor("cltu", [128, 128], F32, kind="ExternalInput").ap()
    d_cexp = nc.dram_tensor("cexp", [128, E], F32, kind="ExternalInput").ap()
    d_cids = nc.dram_tensor("cids", [128, TPC], I32, kind="ExternalInput").ap()
    d_out = nc.dram_tensor("out", [SEQ, D], F32, kind="ExternalOutput").ap()
    d_s1 = nc.dram_tensor("s1scr", [E, HC, 128, KT * 128], BF16).ap()
    d_s2 = nc.dram_tensor("s2scr", [E, HC, 128, D], BF16).ap()
    d_swin = nc.dram_tensor("swinscr", [KT, 128, 2 * D], BF16).ap()
    d_swout = nc.dram_tensor("swoutscr", [KT, 128, D], BF16).ap()
    d_xn2s = nc.dram_tensor("xn2scr", [CHUNK, D], BF16).ap()
    d_x1s = nc.dram_tensor("x1scrf", [CHUNK, D], F32).ap()
    d_stage = nc.dram_tensor("stagescr", [E * CAP, D], BF16).ap()
    d_glist = nc.dram_tensor("glistscr", [E * CAP, 1], I32).ap()

    A = nc.alloc_sbuf_tensor
    winbuf = A("winbuf", [128, 2 * 2 * D], BF16)
    woutbuf = A("woutbuf", [128, 2 * D], BF16)
    gw_sb = A("gw_sb", [128, KT * E], BF16)
    binrow = A("binrow", [1, 2 * D], BF16)
    boutrow = A("boutrow", [1, D], BF16)
    gbrow = A("gbrow", [1, E], BF16)
    b2r = A("b2r", [E, D], BF16)
    NB1 = max(1, (E * HC) // 128)
    B1R = min(128, E * HC)
    b1r = A("b1r", [B1R, NB1 * 128], F32)
    b1c = A("b1c", [128, E * HC], F32)
    at_sb = A("at_sb", [128, 128], BF16)
    dec_sb = A("dec_sb", [1, 128], BF16)
    c128_sb = A("c128_sb", [1, 1], BF16)
    id_sb = A("id_sb", [128, 128], F32)
    ltu_sb = A("ltu_sb", [128, 128], BF16)
    cexp_sb = A("cexp_sb", [128, E], F32)
    cids_sb = A("cids_sb", [128, TPC], I32)
    ones_sb = A("ones_sb", [1, 128], BF16)
    onesc_sb = A("onesc_sb", [128, 1], BF16)
    eps_c = A("eps_c", [128, 1], F32)
    nhalf_c = A("nhalf_c", [128, 1], F32)
    sx_sb = A("sx_sb", [128, 2 * D], F32)
    xa_sb = A("xa_sb", [128, 4 * D], F32)
    x1r_sb = A("x1r_sb", [128, 2 * D], F32)
    f32p = A("f32p", [128, 8 * D], F32)
    trp = A("trp", [128, 2 * D], BF16)
    xn2tt = A("xn2tt", [128, 2 * KT * 128], BF16)
    sg_sb = A("sg_sb", [128, D], F32)
    sh_sb = A("sh_sb", [128, D], F32)
    xg_sb = A("xg_sb", [128, 2 * D], BF16)
    mv_sb = A("mv_sb", [128, 2 * TPC * 2], F32)
    rstd_sb = A("rstd_sb", [128, 2 * TPC], F32)
    negmr_sb = A("negmr_sb", [128, 2 * TPC], F32)
    stats_sb = A("stats_sb", [128, 12], F32)
    g_sb = A("g_sb", [128, E], F32)
    m_sb = A("m_sb", [128, TPC * E], F32)
    mt_sb = A("mt_sb", [E, TPC * 128], BF16)
    gsh_sb = A("gsh_sb", [128, E], F32)
    ez_sb = A("ez_sb", [128, E], F32)
    ismax_sb = A("ismax_sb", [128, E], F32)
    wm_sb = A("wm_sb", [128, E], F32)
    sel_sb = A("sel_sb", [128, E], F32)
    sel1_sb = A("sel1_sb", [128, E], F32)
    sel1t_sb = A("sel1t_sb", [128, TPC * E], F32)
    selu_sb = A("selu_sb", [128, E], F32)
    selub_sb = A("selub_sb", [128, TPC * E], BF16)
    s1_sb = A("s1_sb", [128, 1], F32)
    s2_sb = A("s2_sb", [128, 1], F32)
    s3_sb = A("s3_sb", [128, 1], F32)
    m12_sb = A("m12_sb", [128, 2 * TPC], F32)
    e12_sb = A("e12_sb", [128, 2 * TPC], F32)
    rank_sb = A("rank_sb", [128, E], F32)
    r12_sb = A("r12_sb", [128, 2 * TPC], F32)
    idxf_sb = A("idxf_sb", [128, 2 * TPC], F32)
    idxi_sb = A("idxi_sb", [128, 2 * TPC], I32)
    cnt_sb = A("cnt_sb", [1, TPC * E], BF16)
    idxg_sb = A("idxg_sb", [128, 2 * GT], I32)
    xn2gt_sb = A("xn2gt_sb", [128, 2 * KT * CAP], BF16)
    wstage_sb = A("wstage_sb", [128, 2 * D], BF16)
    gst_sb = A("gst_sb", [128, 2 * D], BF16)
    carry_sb = A("carry_sb", [1, 2 * D], BF16)
    ht_sb = A("ht_sb", [128, HC * CAP], BF16)
    W1S, W2S = 8, 6
    w1buf = A("w1buf", [128, W1S * KT * 128], BF16)
    w2buf = A("w2buf", [128, W2S * D], BF16)
    zeroi_sb = A("zeroi_sb", [128, (E * CAP + 127) // 128], I32)

    P1 = nc.alloc_psum_tensor("P1", [128, 2048], F32)
    P2 = nc.alloc_psum_tensor("P2", [128, 2048], F32)
    P1K = ["P1a", "P1b", "P1c", "P1d"]
    P2K = ["P2a", "P2b", "P2c", "P2d"]

    S = Sched(nc)

    def mm(out_ap, lhsT, rhs, start, stop, reads, writes):
        S.emit("pe",
               lambda e, o=out_ap, l=lhsT, r=rhs, s=start, t=stop:
               e.matmul(o, lhsT=l, rhs=r, start=s, stop=t),
               reads=reads, writes=writes)

    def tr(out_ap, in_ap, reads, writes, nk=128):
        S.emit("pe",
               lambda e, o=out_ap, i=in_ap, nk=nk:
               e.transpose(o, i, id_sb[0:nk, 0:nk]),
               reads=list(reads) + ["id"], writes=writes)

    def tr_set(P, pkc, pkd, src_ap, src_k, dst_ap, dst_k, eng="dve"):
        for half in range(2):
            pk = pkc if half == 0 else pkd
            for k in range(4 * half, 4 * half + 4):
                tr(P[:, 1024 + k * 128:1024 + (k + 1) * 128],
                   src_ap[:, k * 128:(k + 1) * 128],
                   reads=[src_k], writes=[pk])
            hw = 512
            dview = dst_ap[:, half * hw:(half + 1) * hw]
            if eng == "dve":
                S.emit("dve", lambda e, o=dview, half=half: e.tensor_copy(
                    out=o, in_=P[:, 1024 + half * hw:1024 + (half + 1) * hw]),
                    reads=[pk], writes=[dst_k])
            else:
                S.emit("act", lambda e, o=dview, half=half: e.copy(
                    out=o, in_=P[:, 1024 + half * hw:1024 + (half + 1) * hw]),
                    reads=[pk], writes=[dst_k])

    def cast_load(dst_ap, src_ap, wkey):
        S.emit("qg", lambda e, o=dst_ap, i=src_ap: e.dma_start(out=o, in_=i),
               reads=(), writes=[wkey], dma=True, dsem="ini_g", no_waw=False)

    cast_load(gw_sb[:].rearrange("p (k e) -> p k e", k=KT),
              d_gw.rearrange("(k p) e -> p k e", p=128), "gw")
    cast_load(binrow[:], d_bin[:], "binrow")
    cast_load(boutrow[:], d_bout[:], "boutrow")
    cast_load(gbrow[:], d_gb[:], "gbrow")
    cast_load(b2r[:], d_b2[:], "b2r")
    cast_load(at_sb[:], d_at[:], "at")
    cast_load(dec_sb[:], d_dec[:], "dec")
    cast_load(c128_sb[:], d_c128[:], "c128")
    cast_load(ltu_sb[:], d_ltu[:], "ltu")
    S.emit("qs", lambda e: e.dma_start(out=id_sb[:], in_=d_id[:]),
           reads=(), writes=["id"], dma=True, dsem="ini_s")
    S.emit("qs", lambda e: e.dma_start(out=cexp_sb[:], in_=d_cexp[:]),
           reads=(), writes=["cexp"], dma=True, dsem="ini_s")
    S.emit("qs", lambda e: e.dma_start(out=cids_sb[:], in_=d_cids[:]),
           reads=(), writes=["cids"], dma=True, dsem="ini_s")
    for i in range(NB1):
        S.emit("qs",
               lambda e, i=i: e.dma_start(
                   out=b1r[:, i * 128:(i + 1) * 128],
                   in_=d_b1[i * 128:i * 128 + B1R, :]),
               reads=(), writes=["b1r%d" % i], dma=True, dsem="ini_s")
    S.emit("dve", lambda e: e.memset(ones_sb[:], 1.0), writes=["ones"])
    S.emit("dve", lambda e: e.memset(onesc_sb[:], 1.0), writes=["onesc"])
    S.emit("dve", lambda e: e.memset(eps_c[:], EPS), writes=["eps"])
    S.emit("dve", lambda e: e.memset(nhalf_c[:], -0.5), writes=["nhalf"])
    S.emit("dve", lambda e: e.memset(zeroi_sb[:], 0), writes=["zeroi"])
    for _ in range(64):
        mm(P2[0:128, 0:128], id_sb[:], id_sb[:], start=True, stop=True,
           reads=["id"], writes=["P2a"])
    S.emit("qg", lambda e: e.dma_start(
        out=d_glist.rearrange("(p a) b -> p (a b)", p=128), in_=zeroi_sb[:]),
        reads=["zeroi"], writes=["glist"], dma=True, dsem="gli")
    for i in range(NB1):
        tr(P2[0:128, 0:B1R], b1r[0:B1R, i * 128:(i + 1) * 128],
           reads=["b1r%d" % i], writes=["P2a"], nk=B1R)
        S.emit("dve",
               lambda e, i=i: e.tensor_copy(
                   out=b1c[:, i * 128:i * 128 + B1R], in_=P2[0:128, 0:B1R]),
               reads=["P2a"], writes=["b1c"])

    fcnt = [0]
    tcnt = [0]

    def f32slot():
        s = fcnt[0] % 8
        fcnt[0] += 1
        return f32p[:, s * D:(s + 1) * D], "f32p%d" % s

    def trslot():
        s = tcnt[0] % 2
        tcnt[0] += 1
        return trp[:, s * D:(s + 1) * D], "trp%d" % s

    def emit_B(c, tt, defer):
        g = c * TPC + tt
        sx = tt % 2
        sxk = "sx%d" % sx
        sxap = sx_sb[:, sx * D:(sx + 1) * D]
        em = S.defer if defer else S.emit
        em("qs", lambda e, g=g, o=sxap: e.dma_start(
            out=o, in_=d_x[g * 128:(g + 1) * 128, :]),
            reads=(), writes=[sxk], dma=True, dsem="sxd%d" % sx)
        em("dve", lambda e, x=sxap: e.bn_stats(
            out=stats_sb[:, 0:6], in_=x[:, 0:512]),
            reads=[sxk], writes=["stats"])
        em("dve", lambda e, x=sxap: e.bn_stats(
            out=stats_sb[:, 6:12], in_=x[:, 512:1024]),
            reads=[sxk], writes=["stats"])
        em("dve", lambda e, tt=tt: e.bn_aggr(
            out=mv_sb[:, 2 * tt:2 * tt + 2], in_=stats_sb[:]),
            reads=["stats"], writes=["mv%d" % tt])
        em("act", lambda e, tt=tt: e.activation(
            out=rstd_sb[:, tt:tt + 1], in_=mv_sb[:, 2 * tt + 1:2 * tt + 2],
            func=AF.Ln, bias=eps_c[:]),
            reads=["mv%d" % tt, "eps"], writes=["rstd%d" % tt])
        em("act", lambda e, tt=tt: e.activation(
            out=rstd_sb[:, tt:tt + 1], in_=rstd_sb[:, tt:tt + 1],
            func=AF.Exp, scale=nhalf_c[:]),
            reads=["rstd%d" % tt, "nhalf"], writes=["rstd%d" % tt])
        em("dve", lambda e, tt=tt: e.scalar_tensor_tensor(
            out=negmr_sb[:, tt:tt + 1], in0=mv_sb[:, 2 * tt:2 * tt + 1],
            scalar=-1.0, in1=rstd_sb[:, tt:tt + 1],
            op0=ALU.mult, op1=ALU.mult),
            reads=["mv%d" % tt, "rstd%d" % tt], writes=["negmr%d" % tt])

    def emit_route(c, tt):
        em = S.emit
        xsl = tt % 2
        xn2t_k = "xn2tt%d" % xsl
        xb = xsl * KT * 128
        mm(P1[0:128, 1024:1024 + E], ones_sb[:], gbrow[:], True, False,
           reads=["ones", "gbrow"], writes=["P1c"])
        for k in range(KT):
            mm(P1[0:128, 1024:1024 + E],
               xn2tt[:, xb + k * 128:xb + (k + 1) * 128],
               gw_sb[:, k * E:(k + 1) * E], False, k == KT - 1,
               reads=[xn2t_k, "gw"], writes=["P1c"])
        em("dve", lambda e: e.tensor_copy(
            out=g_sb[:], in_=P1[0:128, 1024:1024 + E]),
            reads=["P1c"], writes=["g"])
        em("dve", lambda e: e.reduce_max(
            out=s1_sb[:], in_=g_sb[:], axis=AX.X), reads=["g"], writes=["s1"])
        em("dve", lambda e: e.tensor_scalar(
            out=gsh_sb[:], in0=g_sb[:], scalar1=s1_sb[:], scalar2=None,
            op0=ALU.subtract), reads=["g", "s1"], writes=["gsh"])
        em("act", lambda e: e.activation(
            out=ez_sb[:], in_=gsh_sb[:], func=AF.Exp),
            reads=["gsh"], writes=["ez"])
        em("dve", lambda e: e.reduce_sum(
            out=s2_sb[:], in_=ez_sb[:], axis=AX.X), reads=["ez"], writes=["s2"])
        em("dve", lambda e: e.reciprocal(
            out=s2_sb[:], in_=s2_sb[:]), reads=["s2"], writes=["s2"])
        wap = m_sb[:, tt * E:(tt + 1) * E]
        wk = "m%d" % tt
        em("dve", lambda e, o=wap: e.tensor_scalar(
            out=o, in0=ez_sb[:], scalar1=s2_sb[:], scalar2=None,
            op0=ALU.mult), reads=["ez", "s2"], writes=[wk])
        em("dve", lambda e, w=wap: e.reduce_max(
            out=s1_sb[:], in_=w, axis=AX.X), reads=[wk], writes=["s1"])
        em("dve", lambda e, w=wap: e.tensor_scalar(
            out=sel1_sb[:], in0=w, scalar1=s1_sb[:], scalar2=None,
            op0=ALU.is_ge), reads=[wk, "s1"], writes=["sel1"])
        em("dve", lambda e, w=wap: e.scalar_tensor_tensor(
            out=wm_sb[:], in0=sel1_sb[:], scalar=-1e9, in1=w,
            op0=ALU.mult, op1=ALU.add),
            reads=["sel1", wk], writes=["wm"])
        em("dve", lambda e: e.reduce_max(
            out=s3_sb[:], in_=wm_sb[:], axis=AX.X), reads=["wm"], writes=["s3"])
        em("dve", lambda e, w=wap: e.tensor_scalar(
            out=sel_sb[:], in0=w, scalar1=s3_sb[:], scalar2=None,
            op0=ALU.is_ge), reads=[wk, "s3"], writes=["sel"])
        em("dve", lambda e: e.tensor_copy(
            out=selu_sb[:], in_=sel_sb[:]), reads=["sel"], writes=["selu"])
        em("dve", lambda e, w=wap, o=wap: e.tensor_tensor(
            out=o, in0=w, in1=sel_sb[:], op=ALU.mult),
            reads=[wk, "sel"], writes=[wk])
        em("dve", lambda e, w=wap, tt=tt: e.reduce_max(
            out=m12_sb[:, 2 * tt:2 * tt + 1], in_=w, axis=AX.X),
            reads=[wk], writes=["m12_%d" % tt])
        em("dve", lambda e, w=wap: e.reduce_sum(
            out=s2_sb[:], in_=w, axis=AX.X), reads=[wk], writes=["s2"])
        em("dve", lambda e, tt=tt: e.scalar_tensor_tensor(
            out=m12_sb[:, 2 * tt + 1:2 * tt + 2],
            in0=m12_sb[:, 2 * tt:2 * tt + 1], scalar=-1.0, in1=s2_sb[:],
            op0=ALU.mult, op1=ALU.add),
            reads=["m12_%d" % tt, "s2"], writes=["m12_%d" % tt])
        em("dve", lambda e: e.tensor_tensor(
            out=ismax_sb[:], in0=sel1_sb[:], in1=cexp_sb[:], op=ALU.mult),
            reads=["sel1", "cexp"], writes=["ismax"])
        em("dve", lambda e, tt=tt: e.reduce_sum(
            out=e12_sb[:, 2 * tt:2 * tt + 1], in_=ismax_sb[:], axis=AX.X),
            reads=["ismax"], writes=["e12_%d" % tt])
        em("dve", lambda e: e.tensor_tensor(
            out=ismax_sb[:], in0=selu_sb[:], in1=cexp_sb[:], op=ALU.mult),
            reads=["selu", "cexp"], writes=["ismax"])
        em("dve", lambda e: e.reduce_sum(
            out=s2_sb[:], in_=ismax_sb[:], axis=AX.X),
            reads=["ismax"], writes=["s2"])
        em("dve", lambda e, tt=tt: e.scalar_tensor_tensor(
            out=e12_sb[:, 2 * tt + 1:2 * tt + 2],
            in0=e12_sb[:, 2 * tt:2 * tt + 1], scalar=-1.0, in1=s2_sb[:],
            op0=ALU.mult, op1=ALU.add),
            reads=["e12_%d" % tt, "s2"], writes=["e12_%d" % tt])
        sbap = selub_sb[:, tt * E:(tt + 1) * E]
        sbk = "selub%d" % tt
        em("dve", lambda e, o=sbap: e.tensor_copy(out=o, in_=selu_sb[:]),
           reads=["selu"], writes=[sbk])
        em("dve", lambda e, tt=tt: e.tensor_copy(
            out=sel1t_sb[:, tt * E:(tt + 1) * E], in_=sel1_sb[:]),
            reads=["sel1"], writes=["sel1t%d" % tt])

    def emit_route_b(c, tt):
        em = S.emit
        sbap = selub_sb[:, tt * E:(tt + 1) * E]
        sbk = "selub%d" % tt
        s1ap = sel1t_sb[:, tt * E:(tt + 1) * E]
        s1k = "sel1t%d" % tt
        S.emit("pe", lambda e, i=sbap: e.matmul(
            P1[0:1, 1536:1536 + E], lhsT=onesc_sb[:], rhs=i,
            start=True, stop=True),
            reads=["onesc", sbk], writes=["P1d"])
        em("dve", lambda e, tt=tt: e.tensor_copy(
            out=cnt_sb[0:1, tt * E:(tt + 1) * E], in_=P1[0:1, 1536:1536 + E]),
            reads=["P1d"], writes=["cnt%d" % tt])
        S.emit("pe", lambda e, i=sbap: e.matmul(
            P1[0:128, 1040:1040 + E], lhsT=ltu_sb[:], rhs=i,
            start=True, stop=(tt == 0)),
            reads=["ltu", sbk], writes=["P1c"])
        for jp in range(tt):
            S.emit("pe", lambda e, jp=jp: e.matmul(
                P1[0:128, 1040:1040 + E], lhsT=ones_sb[:],
                rhs=cnt_sb[0:1, jp * E:(jp + 1) * E],
                start=False, stop=(jp == tt - 1)),
                reads=["ones", "cnt%d" % jp], writes=["P1c"])
        em("dve", lambda e: e.tensor_copy(
            out=rank_sb[:], in_=P1[0:128, 1040:1040 + E]),
            reads=["P1c"], writes=["rank"])
        em("dve", lambda e, a=s1ap: e.tensor_tensor(
            out=ismax_sb[:], in0=a, in1=rank_sb[:], op=ALU.mult),
            reads=[s1k, "rank"], writes=["ismax"])
        em("dve", lambda e, tt=tt: e.reduce_sum(
            out=r12_sb[:, 2 * tt:2 * tt + 1], in_=ismax_sb[:], axis=AX.X),
            reads=["ismax"], writes=["r12_%d" % tt])
        em("dve", lambda e, a=sbap: e.tensor_tensor(
            out=ismax_sb[:], in0=a, in1=rank_sb[:], op=ALU.mult),
            reads=[sbk, "rank"], writes=["ismax"])
        em("dve", lambda e: e.reduce_sum(
            out=s2_sb[:], in_=ismax_sb[:], axis=AX.X),
            reads=["ismax"], writes=["s2"])
        em("dve", lambda e, tt=tt: e.scalar_tensor_tensor(
            out=r12_sb[:, 2 * tt + 1:2 * tt + 2],
            in0=r12_sb[:, 2 * tt:2 * tt + 1], scalar=-1.0, in1=s2_sb[:],
            op0=ALU.mult, op1=ALU.add),
            reads=["r12_%d" % tt, "s2"], writes=["r12_%d" % tt])
        em("dve", lambda e, tt=tt: e.tensor_scalar(
            out=r12_sb[:, 2 * tt:2 * tt + 2],
            in0=r12_sb[:, 2 * tt:2 * tt + 2],
            scalar1=float(CAP - 1), scalar2=None, op0=ALU.min),
            reads=["r12_%d" % tt], writes=["r12_%d" % tt])
        for s in range(2):
            em("dve", lambda e, tt=tt, s=s: e.scalar_tensor_tensor(
                out=idxf_sb[:, 2 * tt + s:2 * tt + s + 1],
                in0=e12_sb[:, 2 * tt + s:2 * tt + s + 1],
                scalar=float(CAP),
                in1=r12_sb[:, 2 * tt + s:2 * tt + s + 1],
                op0=ALU.mult, op1=ALU.add),
                reads=["e12_%d" % tt, "r12_%d" % tt],
                writes=["idxf_%d" % tt])
        em("dve", lambda e, tt=tt: e.tensor_copy(
            out=idxi_sb[:, 2 * tt:2 * tt + 2],
            in_=idxf_sb[:, 2 * tt:2 * tt + 2]),
            reads=["idxf_%d" % tt], writes=["idxi%d" % tt])
        for s in range(2):
            S.emit("qg", lambda e, tt=tt, s=s: e.indirect_dma_start(
                out=d_glist[:],
                out_offset=IndirectOffsetOnAxis(
                    ap=idxi_sb[:, 2 * tt + s:2 * tt + s + 1], axis=0),
                in_=cids_sb[:, tt:tt + 1], in_offset=None),
                reads=["idxi%d" % tt, "cids"], writes=["glist"],
                dma=True, dsem="gsc", no_waw=True)

    attn_state = {}

    def emit_attn_s1(c, tt):
        g = c * TPC + tt
        xk = "xa%d" % (tt % 4)
        xap = xa_sb[:, (tt % 4) * D:((tt % 4) + 1) * D]
        S.emit("qs", lambda e, g=g, o=xap: e.dma_start(
            out=o, in_=d_x[g * 128:(g + 1) * 128, :]),
            reads=(), writes=[xk], dma=True, dsem="xad%d" % (tt % 4))
        xn_ap, xn_k = f32slot()
        S.emit("act", lambda e, x=xap, o=xn_ap, tt=tt: e.activation(
            out=o, in_=x, func=AF.Identity,
            bias=negmr_sb[:, tt:tt + 1], scale=rstd_sb[:, tt:tt + 1]),
            reads=[xk, "negmr%d" % tt, "rstd%d" % tt], writes=[xn_k])
        xnt_ap, xnt_k = trslot()
        tr_set(P2, "P2c", "P2d", xn_ap, xn_k, xnt_ap, xnt_k)
        attn_state[tt] = (xap, xk, xnt_ap, xnt_k)

    def emit_attn_s2pair(c, p):
        first_chunk = (c == 0)
        tiles = [p, p + 1]
        P_of = {p: (P1, P1K), p + 1: (P2, P2K)}
        for tt in tiles:
            P, PK = P_of[tt]
            for nch in range(4):
                mm(P[:, nch * 512:(nch + 1) * 512], ones_sb[:],
                   binrow[:, nch * 512:(nch + 1) * 512], True, False,
                   reads=["ones", "binrow"], writes=[PK[nch]])
        for k in range(KT):
            wsl = k % 2
            wks = "winb%d" % wsl
            wap_ = winbuf[:, wsl * 2 * D:(wsl + 1) * 2 * D]
            if first_chunk and p == 0:
                S.emit("qg", lambda e, k=k, o=wap_: e.dma_start(
                    out=o, in_=d_win[k * 128:(k + 1) * 128, :]),
                    reads=(), writes=[wks], dma=True, dsem="wind%d" % wsl)
                S.emit("qs", lambda e, k=k, i=wap_: e.dma_start(
                    out=d_swin[k, :, :], in_=i),
                    reads=[wks], writes=["swin%d" % k], dma=True,
                    dsem="wins%d" % wsl)
            else:
                S.emit("qs", lambda e, k=k, o=wap_: e.dma_start(
                    out=o, in_=d_swin[k, :, :]),
                    reads=["swin%d" % k], writes=[wks], dma=True,
                    dsem="winq%d" % wsl)
            for tt in tiles:
                P, PK = P_of[tt]
                xnt_ap, xnt_k = attn_state[tt][2], attn_state[tt][3]
                for nch in range(4):
                    mm(P[:, nch * 512:(nch + 1) * 512],
                       xnt_ap[:, k * 128:(k + 1) * 128],
                       wap_[:, nch * 512:(nch + 1) * 512],
                       False, k == KT - 1,
                       reads=[xnt_k, wks], writes=[PK[nch]])
        for tt in tiles:
            P, PK = P_of[tt]
            S.emit("act", lambda e, P=P: e.activation(
                out=sg_sb[:], in_=P[:, 0:1024], func=AF.Sigmoid),
                reads=[PK[0], PK[1]], writes=["sg"])
            S.emit("act", lambda e, P=P: e.activation(
                out=sh_sb[:], in_=P[:, 1024:2048], func=AF.Sigmoid),
                reads=[PK[2], PK[3]], writes=["sh"])
            S.emit("dve", lambda e, P=P: e.tensor_tensor(
                out=sh_sb[:], in0=sh_sb[:], in1=P[:, 1024:2048], op=ALU.mult),
                reads=["sh", PK[2], PK[3]], writes=["sh"])
            xg_ap = xg_sb[:, (tt % 2) * D:((tt % 2) + 1) * D]
            xg_k = "xg%d" % (tt % 2)
            S.emit("dve", lambda e, o=xg_ap: e.tensor_tensor(
                out=o, in0=sg_sb[:], in1=sh_sb[:], op=ALU.mult),
                reads=["sg", "sh"], writes=[xg_k])

    def emit_attn_s3(c, tt):
        g = c * TPC + tt
        xg_ap = xg_sb[:, (tt % 2) * D:((tt % 2) + 1) * D]
        xg_k = "xg%d" % (tt % 2)
        cprev = carry_sb[0:1, ((g + 1) % 2) * D:((g + 1) % 2 + 1) * D]
        cprev_k = "cr%d" % ((g + 1) % 2)
        ccur = carry_sb[0:1, (g % 2) * D:(g % 2 + 1) * D]
        ccur_k = "cr%d" % (g % 2)
        for nch in range(2):
            mm(P2[:, nch * 512:(nch + 1) * 512], at_sb[:],
               xg_ap[:, nch * 512:(nch + 1) * 512], True, g == 0,
               reads=["at", xg_k], writes=[P2K[nch]])
            if g > 0:
                mm(P2[:, nch * 512:(nch + 1) * 512], dec_sb[:],
                   cprev[:, nch * 512:(nch + 1) * 512], False, True,
                   reads=["dec", cprev_k], writes=[P2K[nch]])
        for nch in range(2):
            mm(P2[0:1, 1024 + nch * 512:1024 + (nch + 1) * 512],
               at_sb[:, 127:128], xg_ap[:, nch * 512:(nch + 1) * 512],
               True, g == 0,
               reads=["at", xg_k], writes=[P2K[2 + nch]])
            if g > 0:
                mm(P2[0:1, 1024 + nch * 512:1024 + (nch + 1) * 512],
                   c128_sb[:], cprev[:, nch * 512:(nch + 1) * 512],
                   False, True,
                   reads=["c128", cprev_k], writes=[P2K[2 + nch]])
        S.emit("act", lambda e, o=ccur: e.activation(
            out=o, in_=P2[0:1, 1024:2048], func=AF.Copy),
            reads=["P2c", "P2d"], writes=[ccur_k])
        hid_ap, hid_k = f32slot()
        S.emit("dve", lambda e, o=hid_ap: e.tensor_copy(
            out=o, in_=P2[:, 0:1024]),
            reads=["P2a", "P2b"], writes=[hid_k])
        ht_ap, ht_k = trslot()
        tr_set(P1, "P1c", "P1d", hid_ap, hid_k, ht_ap, ht_k)
        attn_state[(tt, "ht")] = (ht_ap, ht_k)

    def emit_attn_s4a_pair(c, p):
        first_chunk = (c == 0)
        tiles = [p, p + 1]
        P_of = {p: (P1, P1K), p + 1: (P2, P2K)}
        for tt in tiles:
            P, PK = P_of[tt]
            for nch in range(2):
                mm(P[:, nch * 512:(nch + 1) * 512], ones_sb[:],
                   boutrow[:, nch * 512:(nch + 1) * 512], True, False,
                   reads=["ones", "boutrow"], writes=[PK[nch]])
        for k in range(KT):
            wsl = k % 2
            wks = "woutb%d" % wsl
            wap_ = woutbuf[:, wsl * D:(wsl + 1) * D]
            if first_chunk and p == 0:
                S.emit("qg", lambda e, k=k, o=wap_: e.dma_start(
                    out=o, in_=d_wout[k * 128:(k + 1) * 128, :]),
                    reads=(), writes=[wks], dma=True, dsem="wod%d" % wsl)
                S.emit("qs", lambda e, k=k, i=wap_: e.dma_start(
                    out=d_swout[k, :, :], in_=i),
                    reads=[wks], writes=["swout%d" % k], dma=True,
                    dsem="wos%d" % wsl)
            else:
                S.emit("qs", lambda e, k=k, o=wap_: e.dma_start(
                    out=o, in_=d_swout[k, :, :]),
                    reads=["swout%d" % k], writes=[wks], dma=True,
                    dsem="woq%d" % wsl)
            for tt in tiles:
                P, PK = P_of[tt]
                ht_ap, ht_k = attn_state[(tt, "ht")]
                for nch in range(2):
                    mm(P[:, nch * 512:(nch + 1) * 512],
                       ht_ap[:, k * 128:(k + 1) * 128],
                       wap_[:, nch * 512:(nch + 1) * 512],
                       False, k == KT - 1,
                       reads=[ht_k, wks], writes=[PK[nch]])
        for tt in tiles:
            P, PK = P_of[tt]
            xap, xk = attn_state[tt][0], attn_state[tt][1]
            x1ap = x1r_sb[:, (tt % 2) * D:((tt % 2) + 1) * D]
            x1k = "x1r%d" % (tt % 2)
            S.emit("dve", lambda e, x=xap, o=x1ap, P=P: e.tensor_tensor(
                out=o, in0=P[:, 0:1024], in1=x, op=ALU.add),
                reads=[PK[0], PK[1], xk], writes=[x1k])

    def emit_attn_s4b(c, tt):
        g = c * TPC + tt
        x1ap = x1r_sb[:, (tt % 2) * D:((tt % 2) + 1) * D]
        x1k = "x1r%d" % (tt % 2)
        S.emit("dve", lambda e, i=x1ap: e.bn_stats(
            out=stats_sb[:, 0:6], in_=i[:, 0:512]),
            reads=[x1k], writes=["stats"])
        S.emit("dve", lambda e, i=x1ap: e.bn_stats(
            out=stats_sb[:, 6:12], in_=i[:, 512:1024]),
            reads=[x1k], writes=["stats"])
        S.emit("dve", lambda e, tt=tt: e.bn_aggr(
            out=mv_sb[:, 2 * TPC + 2 * tt:2 * TPC + 2 * tt + 2],
            in_=stats_sb[:]),
            reads=["stats"], writes=["mv2_%d" % tt])
        S.emit("act", lambda e, tt=tt: e.activation(
            out=rstd_sb[:, TPC + tt:TPC + tt + 1],
            in_=mv_sb[:, 2 * TPC + 2 * tt + 1:2 * TPC + 2 * tt + 2],
            func=AF.Ln, bias=eps_c[:]),
            reads=["mv2_%d" % tt, "eps"], writes=["rstd2_%d" % tt])
        S.emit("act", lambda e, tt=tt: e.activation(
            out=rstd_sb[:, TPC + tt:TPC + tt + 1],
            in_=rstd_sb[:, TPC + tt:TPC + tt + 1],
            func=AF.Exp, scale=nhalf_c[:]),
            reads=["rstd2_%d" % tt, "nhalf"], writes=["rstd2_%d" % tt])
        S.emit("dve", lambda e, tt=tt: e.scalar_tensor_tensor(
            out=negmr_sb[:, TPC + tt:TPC + tt + 1],
            in0=mv_sb[:, 2 * TPC + 2 * tt:2 * TPC + 2 * tt + 1],
            scalar=-1.0, in1=rstd_sb[:, TPC + tt:TPC + tt + 1],
            op0=ALU.mult, op1=ALU.mult),
            reads=["mv2_%d" % tt, "rstd2_%d" % tt], writes=["negmr2_%d" % tt])
        xn2_ap, xn2_k = f32slot()
        S.emit("act", lambda e, o=xn2_ap, i=x1ap, tt=tt: e.activation(
            out=o, in_=i, func=AF.Identity,
            bias=negmr_sb[:, TPC + tt:TPC + tt + 1],
            scale=rstd_sb[:, TPC + tt:TPC + tt + 1]),
            reads=[x1k, "negmr2_%d" % tt, "rstd2_%d" % tt],
            writes=[xn2_k])
        S.emit("act", lambda e, i=x1ap, tt=tt: e.dma_start(
            out=d_x1s[tt * 128:(tt + 1) * 128, :], in_=i),
            reads=[x1k], writes=["x1s%d" % tt], dma=True,
            dsem="x1sd%d" % (tt % 2))
        xsl = tt % 2
        tr_set(P1, "P1c", "P1d", xn2_ap,
               xn2_k, xn2tt[:, xsl * KT * 128:(xsl + 1) * KT * 128],
               "xn2tt%d" % xsl)
        S.emit("qg", lambda e, i=xn2_ap, tt=tt: e.dma_start(
            out=d_xn2s[tt * 128:(tt + 1) * 128, :], in_=i),
            reads=[xn2_k], writes=["xn2s%d" % tt], dma=True,
            dsem="xsp%d" % (tt % 2))

    gslots = {}

    def emit_gather_dma(c, ex):
        xn2s_keys = ["xn2s%d" % t for t in range(TPC)]
        sl = ex % 2
        slots = []
        for gt in range(GT):
            ig = GT * sl + gt
            S.emit("qg", lambda e, ex=ex, gt=gt, ig=ig: e.dma_start(
                out=idxg_sb[:, ig:ig + 1],
                in_=d_glist[ex * CAP + gt * 128:ex * CAP + (gt + 1) * 128, :]),
                reads=["glist"], writes=["idxgs%d" % sl], dma=True,
                dsem="idg", no_waw=True)
            gap_ap, gap_k = f32slot()
            S.emit("qg", lambda e, ig=ig, o=gap_ap: e.indirect_dma_start(
                out=o, out_offset=None,
                in_=d_xn2s[:],
                in_offset=IndirectOffsetOnAxis(
                    ap=idxg_sb[:, ig:ig + 1], axis=0)),
                reads=xn2s_keys + ["idxgs%d" % sl],
                writes=[gap_k], dma=True, dsem="xng_" + gap_k)
            slots.append((gap_ap, gap_k))
        gslots[ex] = slots

    def emit_gather_tr(c, ex):
        sl = ex % 2
        base = sl * KT * CAP
        for gt in range(GT):
            gap_ap, gap_k = gslots[ex][gt]
            for half in range(2):
                pk = "P2c" if half == 0 else "P2d"
                for k in range(4 * half, 4 * half + 4):
                    tr(P2[:, 1024 + k * 128:1024 + (k + 1) * 128],
                       gap_ap[:, k * 128:(k + 1) * 128],
                       reads=[gap_k], writes=[pk])
                xg_dst = (xn2gt_sb[:, base:base + KT * CAP]
                          .rearrange("p (k t) -> p k t", t=CAP)
                          [:, 4 * half:4 * half + 4, gt * 128:(gt + 1) * 128])
                pview = (P2[:, 1024 + half * 512:1536 + half * 512]
                         .rearrange("p (k t) -> p k t", t=128))
                if (gt + half) % 2 == 0:
                    S.emit("act", lambda e, o=xg_dst, i=pview: e.copy(
                        out=o, in_=i),
                        reads=[pk], writes=["xn2gt%d" % sl])
                else:
                    S.emit("dve", lambda e, o=xg_dst, i=pview: e.tensor_copy(
                        out=o, in_=i),
                        reads=[pk], writes=["xn2gt%d" % sl])

    def emit_moe(c):
        first_chunk = (c == 0)
        emit_gather_dma(c, 0)
        if E > 1:
            emit_gather_dma(c, 1)
        emit_gather_tr(c, 0)
        for ex in range(E):
            if c + 1 < NCH and ex < TPC:
                emit_B(c + 1, ex, defer=True)
            sl = ex % 2
            base = sl * KT * CAP
            if ex + 2 < E:
                emit_gather_dma(c, ex + 2)
            if ex + 1 < E:
                emit_gather_tr(c, ex + 1)
            for hc in range(HC):
                wi = (ex * HC + hc) % W1S
                wk = "w1b%d" % wi
                sk = "s1_%d_%d" % (ex, hc)
                if first_chunk:
                    S.emit("qg",
                           lambda e, ex=ex, hc=hc, wi=wi: e.dma_start(
                               out=w1buf[:, wi * KT * 128:(wi + 1) * KT * 128]
                               .rearrange("p (k q) -> p k q", q=128),
                               in_=d_w1[ex, :, hc * 128:(hc + 1) * 128]
                               .rearrange("(k p) q -> p k q", p=128)),
                           reads=(), writes=[wk], dma=True)
                    S.emit("qs",
                           lambda e, ex=ex, hc=hc, wi=wi: e.dma_start(
                               out=d_s1[ex, hc, :, :],
                               in_=w1buf[:, wi * KT * 128:(wi + 1) * KT * 128]),
                           reads=[wk], writes=[sk], dma=True,
                           dsem="sw1_%d" % wi)
                else:
                    S.emit("qs",
                           lambda e, ex=ex, hc=hc, wi=wi: e.dma_start(
                               out=w1buf[:, wi * KT * 128:(wi + 1) * KT * 128],
                               in_=d_s1[ex, hc, :, :]),
                           reads=[sk], writes=[wk], dma=True,
                           dsem="w1q%d" % wi)
                ps = P1K[hc % 4]
                pof = 512 * (hc % 4)
                for k in range(KT):
                    mm(P1[:, pof:pof + 512],
                       w1buf[:, wi * KT * 128 + k * 128:wi * KT * 128 + (k + 1) * 128],
                       xn2gt_sb[:, base + k * CAP:base + (k + 1) * CAP],
                       k == 0, k == KT - 1,
                       reads=[wk, "xn2gt%d" % sl], writes=[ps])
                S.pump(2)
                hk = "ht%d" % hc
                if hc % 2 == 0:
                    S.emit("act", lambda e, ex=ex, hc=hc, pof=pof: e.activation(
                        out=ht_sb[:, hc * CAP:(hc + 1) * CAP],
                        in_=P1[:, pof:pof + CAP], func=AF.Relu,
                        bias=b1c[:, ex * HC + hc:ex * HC + hc + 1]),
                        reads=[ps, "b1c"], writes=[hk])
                else:
                    S.emit("dve", lambda e, ex=ex, hc=hc, pof=pof: e.tensor_scalar(
                        out=ht_sb[:, hc * CAP:(hc + 1) * CAP],
                        in0=P1[:, pof:pof + CAP],
                        scalar1=b1c[:, ex * HC + hc:ex * HC + hc + 1],
                        scalar2=0.0, op0=ALU.add, op1=ALU.max),
                        reads=[ps, "b1c"], writes=[hk])
            w2slots = [("P1", 0), ("P1", 1024), ("P2", 0), ("P2", 1024)]
            for kt in range(HC):
                wi = kt % W2S
                wk = "w2b%d" % wi
                sk = "s2_%d_%d" % (ex, kt)
                if first_chunk:
                    S.emit("qg",
                           lambda e, ex=ex, kt=kt, wi=wi: e.dma_start(
                               out=w2buf[:, wi * D:(wi + 1) * D],
                               in_=d_w2[ex, kt * 128:(kt + 1) * 128, :]),
                           reads=(), writes=[wk], dma=True)
                    S.emit("qs",
                           lambda e, ex=ex, kt=kt, wi=wi: e.dma_start(
                               out=d_s2[ex, kt, :, :],
                               in_=w2buf[:, wi * D:(wi + 1) * D]),
                           reads=[wk], writes=[sk], dma=True,
                           dsem="sw2_%d" % wi)
                else:
                    S.emit("qs",
                           lambda e, ex=ex, kt=kt, wi=wi: e.dma_start(
                               out=w2buf[:, wi * D:(wi + 1) * D],
                               in_=d_s2[ex, kt, :, :]),
                           reads=[sk], writes=[wk], dma=True,
                           dsem="w2q%d" % wi)
                for rt in range(GT):
                    pn, po = w2slots[rt]
                    P = P1 if pn == "P1" else P2
                    pk = (P1K if pn == "P1" else P2K)
                    for nch in range(2):
                        mm(P[:, po + nch * 512:po + (nch + 1) * 512],
                           ht_sb[:, kt * CAP + rt * 128:kt * CAP + (rt + 1) * 128],
                           w2buf[:, wi * D + nch * 512:wi * D + (nch + 1) * 512],
                           kt == 0, kt == HC - 1,
                           reads=["ht%d" % kt, wk],
                           writes=[pk[po // 512 + nch]])

            for rt in range(GT):
                pn, po = w2slots[rt]
                P = P1 if pn == "P1" else P2
                pk = (P1K if pn == "P1" else P2K)
                ws = rt % 2
                S.emit("act", lambda e, P=P, po=po, ws=ws: e.copy(
                    out=wstage_sb[:, ws * D:(ws + 1) * D],
                    in_=P[:, po:po + 1024]),
                    reads=[pk[po // 512], pk[po // 512 + 1]],
                    writes=["wst%d" % ws])
                S.emit("act", lambda e, ex=ex, rt=rt, ws=ws: e.dma_start(
                    out=d_stage[ex * CAP + rt * 128:ex * CAP + (rt + 1) * 128, :],
                    in_=wstage_sb[:, ws * D:(ws + 1) * D]),
                    reads=["wst%d" % ws], writes=["stage"], dma=True,
                    dsem="stw%d" % ws)

    def emit_combine_pre(c):
        for tt in range(TPC):
            tr(P2[0:E, 1024 + (tt % 2) * 512:1152 + (tt % 2) * 512],
               m_sb[:, tt * E:(tt + 1) * E],
               reads=["m%d" % tt], writes=["P2c" if tt % 2 == 0 else "P2d"])
            S.emit("dve", lambda e, tt=tt: e.tensor_copy(
                out=mt_sb[0:E, tt * 128:(tt + 1) * 128],
                in_=P2[0:E, 1024 + (tt % 2) * 512:1152 + (tt % 2) * 512]),
                reads=["P2c" if tt % 2 == 0 else "P2d"], writes=["mt%d" % tt])

    def emit_combine(c):
        for tt in range(TPC):
            g = c * TPC + tt
            x1ap = x1r_sb[:, (tt % 2) * D:((tt % 2) + 1) * D]
            x1k = "x1r%d" % (tt % 2)
            S.emit("qs", lambda e, tt=tt, o=x1ap: e.dma_start(
                out=o, in_=d_x1s[tt * 128:(tt + 1) * 128, :]),
                reads=["x1s%d" % tt], writes=[x1k], dma=True,
                dsem="x1ld%d" % (tt % 2))
            for s in range(2):
                S.emit("qg", lambda e, tt=tt, s=s: e.indirect_dma_start(
                    out=gst_sb[:, s * D:(s + 1) * D], out_offset=None,
                    in_=d_stage[:],
                    in_offset=IndirectOffsetOnAxis(
                        ap=idxi_sb[:, 2 * tt + s:2 * tt + s + 1], axis=0)),
                    reads=["stage", "idxi%d" % tt], writes=["gst%d" % s],
                    dma=True, dsem="gso%d" % s)
            bo = 1024 * (tt % 2)
            for nch in range(2):
                mm(P2[:, bo + nch * 512:bo + (nch + 1) * 512],
                   mt_sb[0:E, tt * 128:(tt + 1) * 128],
                   b2r[0:E, nch * 512:(nch + 1) * 512], True, True,
                   reads=["mt%d" % tt, "b2r"], writes=[P2K[2 * (tt % 2) + nch]])
            S.emit("dve", lambda e, o=x1ap, bo=bo: e.tensor_tensor(
                out=o, in0=P2[:, bo:bo + 1024], in1=o, op=ALU.add),
                reads=[P2K[2 * (tt % 2)], P2K[2 * (tt % 2) + 1], x1k],
                writes=[x1k])
            for s in range(2):
                S.emit("dve",
                       lambda e, tt=tt, s=s, o=x1ap: e.scalar_tensor_tensor(
                           out=o, in0=gst_sb[:, s * D:(s + 1) * D],
                           scalar=m12_sb[:, 2 * tt + s:2 * tt + s + 1],
                           in1=o, op0=ALU.mult, op1=ALU.add),
                       reads=["gst%d" % s, "m12_%d" % tt, x1k],
                       writes=[x1k])
            S.emit("qs", lambda e, g=g, i=x1ap: e.dma_start(
                out=d_out[g * 128:(g + 1) * 128, :], in_=i),
                reads=[x1k], writes=["outd%d" % g], dma=True,
                dsem="o%d" % (tt % 2))

    for c in range(NCH):
        if c == 0:
            for tt in range(TPC):
                emit_B(0, tt, defer=False)
        for p in range(0, TPC, 2):
            emit_attn_s1(c, p)
            emit_attn_s1(c, p + 1)
            emit_attn_s2pair(c, p)
            emit_attn_s3(c, p)
            emit_attn_s3(c, p + 1)
            emit_attn_s4a_pair(c, p)
            emit_attn_s4b(c, p)
            emit_attn_s4b(c, p + 1)
            emit_route(c, p)
            emit_route(c, p + 1)
            if p >= 4:
                emit_route_b(c, p - 4)
                emit_route_b(c, p - 3)
        for tt in range(TPC - 4, TPC):
            emit_route_b(c, tt)
        emit_combine_pre(c)
        emit_moe(c)
        S.drain()
        emit_combine(c)

    S.finalize_tokens()
    out_totals = [(S.sems["d_o%d" % p], S.totals["d_o%d" % p])
                  for p in range(2)]

    def _final(e, pairs=out_totals):
        for sem, v in pairs:
            e.wait_ge(sem, v)
        return e.nop()

    fin = _Op(_final, "qs", "qs", False)
    S.ops["qs"].append(fin)

    with nc.Block() as blk:
        @blk.tensor
        def _(e):
            S.replay("pe", e)

        @blk.vector
        def _(e):
            S.replay("dve", e)

        @blk.scalar
        def _(e):
            S.replay("act", e)

        @blk.sync
        def _(e):
            S.replay("qs", e)

        @blk.gpsimd
        def _(e):
            S.replay("qg", e)

    return nc


def make_consts(TPC):
    t = np.arange(128)
    AT = np.where(t[None, :] >= t[:, None],
                  INPUT_SCALE * DECAY ** (t[None, :] - t[:, None]), 0.0
                  ).astype(np.float32)
    dec = (DECAY ** (t + 1)).astype(np.float32)[None, :]
    c128 = np.array([[DECAY ** 128]], dtype=np.float32)
    ident = np.eye(128, dtype=np.float32)
    ltu = np.triu(np.ones((128, 128), dtype=np.float32), k=1)
    cexp = np.tile(np.arange(E, dtype=np.float32), (128, 1))
    cids = (np.arange(TPC, dtype=np.int32)[None, :] * 128
            + np.arange(128, dtype=np.int32)[:, None]).astype(np.int32)
    return AT, dec, c128, ident, ltu, cexp, cids


def make_in_maps(x, Win, b_in, Wout, b_out, gw, gb, w1, b1, w2, b2,
                 SEQ=4096, H=4096, CHUNK=1024):
    HC = H // 128
    AT, dec, c128, ident, ltu, cexp, cids = make_consts(CHUNK // 128)
    f32 = lambda a: np.ascontiguousarray(np.asarray(a, dtype=np.float32))
    common = dict(
        Win=f32(Win), bin=f32(b_in).reshape(1, 2 * D),
        Wout=f32(Wout), bout=f32(b_out).reshape(1, D),
        gw=f32(gw), gb=f32(gb).reshape(1, E),
        w1=f32(w1), b1=f32(b1).reshape(E * HC, 128),
        w2=f32(w2), b2=f32(b2),
        cAT=AT, cdec=dec, c128=c128, cid=ident,
        cltu=ltu, cexp=cexp, cids=cids,
    )
    xs = f32(x)
    return [dict(common, x=np.ascontiguousarray(xs[b]))
            for b in range(xs.shape[0])]


def kernel(L0_attn_x, L0_attn_W_in, L0_attn_b_in, L0_attn_W_out, L0_attn_b_out,
           L0_ffn_gate_w, L0_ffn_gate_b, L0_ffn_w1, L0_ffn_b1, L0_ffn_w2,
           L0_ffn_b2):
    nc = build_program(SEQ=4096, CHUNK=1024, H=4096)
    in_maps = make_in_maps(L0_attn_x, L0_attn_W_in, L0_attn_b_in,
                           L0_attn_W_out, L0_attn_b_out, L0_ffn_gate_w,
                           L0_ffn_gate_b, L0_ffn_w1, L0_ffn_b1, L0_ffn_w2,
                           L0_ffn_b2)
    res = run_bass_kernel_spmd(nc, in_maps, core_ids=list(range(N_CORES)))
    out = np.stack([np.asarray(res.results[b]["out"]) for b in range(N_CORES)],
                   axis=0)
    return out.astype(np.float32)


# version 23
# speedup vs baseline: 1.0294x; 1.0294x over previous
import sys
import numpy as np

try:
    import concourse.bass as bass
except ImportError:
    sys.path.insert(0, "/opt/trn_rl_repo")
    import concourse.bass as bass
import concourse.mybir as mybir
from concourse.bass_utils import run_bass_kernel_spmd
from concourse.bass_core_rust import IndirectOffsetOnAxis

F32 = mybir.dt.float32
I32 = mybir.dt.int32
BF16 = mybir.dt.bfloat16
AF = mybir.ActivationFunctionType
ALU = mybir.AluOpType
AX = mybir.AxisListType

N_CORES = 8
D = 1024
KT = D // 128
E = 8
DECAY = 0.9
INPUT_SCALE = 0.1
EPS = 1e-5


class _Op:
    __slots__ = ("fn", "eng", "sem", "needs_inc", "inc_amt", "token", "waits")

    def __init__(self, fn, eng, sem, dma):
        self.fn = fn
        self.eng = eng
        self.sem = sem
        self.needs_inc = bool(dma)
        self.inc_amt = 16 if dma else 1
        self.token = None
        self.waits = []


_SELF_SYNC = {"pe": False, "dve": True, "act": True, "qs": True, "qg": True}


class Sched:

    def __init__(self, nc):
        self.nc = nc
        self.ops = {k: [] for k in _SELF_SYNC}
        self.sems = {k: nc.alloc_semaphore("sem_" + k) for k in _SELF_SYNC}
        self.last_writer = {}
        self.readers = {}
        self.cowriters = {}
        self.fillers = []
        self.grouped_sems = {"d_gsc", "d_idg", "d_xsp0", "d_xsp1",
                             "d_x1sd0", "d_x1sd1", "d_ini_g", "d_ini_s"}
        self.sem_last = {}

    def pump(self, n=1):
        for _ in range(n):
            if not self.fillers:
                return
            self.fillers.pop(0)()

    def drain(self):
        while self.fillers:
            self.fillers.pop(0)()

    def _sem(self, name):
        if name not in self.sems:
            self.sems[name] = self.nc.alloc_semaphore("sem_" + name)
        return self.sems[name]

    def emit(self, eng, fn, reads=(), writes=(), dma=False, dsem=None,
             no_waw=False):
        sem = eng
        if dma:
            sem = "d_" + (dsem if dsem is not None else writes[0])
            self._sem(sem)
        op = _Op(fn, eng, sem, dma)
        deps = []
        for k in reads:
            w = self.last_writer.get(k)
            if w is not None:
                deps.append(w)
            deps.extend(self.cowriters.get(k, ()))
        for k in writes:
            w = self.last_writer.get(k)
            if w is not None:
                deps.append(w)
            deps.extend(self.readers.get(k, ()))
            if not no_waw:
                deps.extend(self.cowriters.get(k, ()))
        for pop in deps:
            if pop is op:
                continue
            if pop.sem in self.grouped_sems:
                pop = self.sem_last[pop.sem]
            if pop.eng == eng and pop.sem == sem and not _SELF_SYNC[eng]:
                continue
            if pop not in op.waits:
                pop.needs_inc = True
                op.waits.append(pop)
        for k in reads:
            self.readers.setdefault(k, []).append(op)
        for k in writes:
            if no_waw:
                self.cowriters.setdefault(k, []).append(op)
            else:
                self.last_writer[k] = op
                self.readers[k] = []
                self.cowriters[k] = []
        if op.inc_amt == 16:
            self.sem_last[sem] = op
        self.ops[eng].append(op)
        return op

    def defer(self, eng, fn, reads=(), writes=(), dma=False, dsem=None):
        self.fillers.append(
            lambda eng=eng, fn=fn, r=tuple(reads), w=tuple(writes), d=dma,
            ds=dsem: self.emit(eng, fn, r, w, dma=d, dsem=ds))

    def finalize_tokens(self):
        counts = {}
        for eng, ops in self.ops.items():
            for op in ops:
                if op.needs_inc:
                    c = counts.get(op.sem, 0) + op.inc_amt
                    counts[op.sem] = c
                    op.token = c
        self.totals = counts

    def replay(self, eng, e):
        watermark = {}
        for op in self.ops[eng]:
            want = {}
            for pop in op.waits:
                assert pop.token is not None
                if pop.token > want.get(pop.sem, -1):
                    want[pop.sem] = pop.token
            for s, v in sorted(want.items()):
                if v > watermark.get(s, 0):
                    e.wait_ge(self.sems[s], v)
                    watermark[s] = v
            instr = op.fn(e)
            if op.needs_inc:
                instr.then_inc(self.sems[op.sem], op.inc_amt)


def build_program(SEQ=4096, CHUNK=1024, H=4096):
    TPC = CHUNK // 128
    NCH = SEQ // CHUNK
    HC = H // 128
    CAP = CHUNK // 2
    GT = CAP // 128

    nc = bass.Bass("TRN2", target_bir_lowering=False)

    d_x = nc.dram_tensor("x", [SEQ, D], F32, kind="ExternalInput").ap()
    d_win = nc.dram_tensor("Win", [D, 2 * D], F32, kind="ExternalInput").ap()
    d_bin = nc.dram_tensor("bin", [1, 2 * D], F32, kind="ExternalInput").ap()
    d_wout = nc.dram_tensor("Wout", [D, D], F32, kind="ExternalInput").ap()
    d_bout = nc.dram_tensor("bout", [1, D], F32, kind="ExternalInput").ap()
    d_gw = nc.dram_tensor("gw", [D, E], F32, kind="ExternalInput").ap()
    d_gb = nc.dram_tensor("gb", [1, E], F32, kind="ExternalInput").ap()
    d_w1 = nc.dram_tensor("w1", [E, D, H], F32, kind="ExternalInput").ap()
    d_b1 = nc.dram_tensor("b1", [E * HC, 128], F32, kind="ExternalInput").ap()
    d_w2 = nc.dram_tensor("w2", [E, H, D], F32, kind="ExternalInput").ap()
    d_b2 = nc.dram_tensor("b2", [E, D], F32, kind="ExternalInput").ap()
    d_at = nc.dram_tensor("cAT", [128, 128], F32, kind="ExternalInput").ap()
    d_dec = nc.dram_tensor("cdec", [1, 128], F32, kind="ExternalInput").ap()
    d_c128 = nc.dram_tensor("c128", [1, 1], F32, kind="ExternalInput").ap()
    d_id = nc.dram_tensor("cid", [128, 128], F32, kind="ExternalInput").ap()
    d_ltu = nc.dram_tensor("cltu", [128, 128], F32, kind="ExternalInput").ap()
    d_cexp = nc.dram_tensor("cexp", [128, E], F32, kind="ExternalInput").ap()
    d_cids = nc.dram_tensor("cids", [128, TPC], I32, kind="ExternalInput").ap()
    d_out = nc.dram_tensor("out", [SEQ, D], F32, kind="ExternalOutput").ap()
    d_s1 = nc.dram_tensor("s1scr", [E, HC, 128, KT * 128], BF16).ap()
    d_s2 = nc.dram_tensor("s2scr", [E, HC, 128, D], BF16).ap()
    d_swin = nc.dram_tensor("swinscr", [KT, 128, 2 * D], BF16).ap()
    d_swout = nc.dram_tensor("swoutscr", [KT, 128, D], BF16).ap()
    d_xn2s = nc.dram_tensor("xn2scr", [CHUNK, D], BF16).ap()
    d_x1s = nc.dram_tensor("x1scrf", [CHUNK, D], F32).ap()
    d_stage = nc.dram_tensor("stagescr", [E * CAP, D], BF16).ap()
    d_glist = nc.dram_tensor("glistscr", [E * CAP, 1], I32).ap()

    A = nc.alloc_sbuf_tensor
    winbuf = A("winbuf", [128, 2 * 2 * D], BF16)
    woutbuf = A("woutbuf", [128, 2 * D], BF16)
    gw_sb = A("gw_sb", [128, KT * E], BF16)
    binrow = A("binrow", [1, 2 * D], BF16)
    boutrow = A("boutrow", [1, D], BF16)
    gbrow = A("gbrow", [1, E], BF16)
    b2r = A("b2r", [E, D], BF16)
    NB1 = max(1, (E * HC) // 128)
    B1R = min(128, E * HC)
    b1r = A("b1r", [B1R, NB1 * 128], F32)
    b1c = A("b1c", [128, E * HC], F32)
    at_sb = A("at_sb", [128, 128], BF16)
    dec_sb = A("dec_sb", [1, 128], BF16)
    c128_sb = A("c128_sb", [1, 1], BF16)
    id_sb = A("id_sb", [128, 128], F32)
    ltu_sb = A("ltu_sb", [128, 128], BF16)
    cexp_sb = A("cexp_sb", [128, E], F32)
    cids_sb = A("cids_sb", [128, TPC], I32)
    ones_sb = A("ones_sb", [1, 128], BF16)
    onesc_sb = A("onesc_sb", [128, 1], BF16)
    eps_c = A("eps_c", [128, 1], F32)
    nhalf_c = A("nhalf_c", [128, 1], F32)
    sx_sb = A("sx_sb", [128, 2 * D], F32)
    xa_sb = A("xa_sb", [128, 4 * D], F32)
    x1r_sb = A("x1r_sb", [128, 2 * D], F32)
    f32p = A("f32p", [128, 8 * D], F32)
    trp = A("trp", [128, 2 * D], BF16)
    xn2tt = A("xn2tt", [128, 2 * KT * 128], BF16)
    sg_sb = A("sg_sb", [128, D], F32)
    sh_sb = A("sh_sb", [128, D], F32)
    xg_sb = A("xg_sb", [128, 2 * D], BF16)
    mv_sb = A("mv_sb", [128, 2 * TPC * 2], F32)
    rstd_sb = A("rstd_sb", [128, 2 * TPC], F32)
    negmr_sb = A("negmr_sb", [128, 2 * TPC], F32)
    stats_sb = A("stats_sb", [128, 12], F32)
    g_sb = A("g_sb", [128, E], F32)
    m_sb = A("m_sb", [128, TPC * E], F32)
    mt_sb = A("mt_sb", [E, TPC * 128], BF16)
    gsh_sb = A("gsh_sb", [128, E], F32)
    ez_sb = A("ez_sb", [128, E], F32)
    ismax_sb = A("ismax_sb", [128, E], F32)
    wm_sb = A("wm_sb", [128, E], F32)
    sel_sb = A("sel_sb", [128, E], F32)
    sel1_sb = A("sel1_sb", [128, E], F32)
    sel1t_sb = A("sel1t_sb", [128, TPC * E], F32)
    selu_sb = A("selu_sb", [128, E], F32)
    selub_sb = A("selub_sb", [128, TPC * E], BF16)
    s1_sb = A("s1_sb", [128, 1], F32)
    s2_sb = A("s2_sb", [128, 1], F32)
    s3_sb = A("s3_sb", [128, 1], F32)
    m12_sb = A("m12_sb", [128, 2 * TPC], F32)
    e12_sb = A("e12_sb", [128, 2 * TPC], F32)
    rank_sb = A("rank_sb", [128, E], F32)
    r12_sb = A("r12_sb", [128, 2 * TPC], F32)
    idxf_sb = A("idxf_sb", [128, 2 * TPC], F32)
    idxi_sb = A("idxi_sb", [128, 2 * TPC], I32)
    cnt_sb = A("cnt_sb", [1, TPC * E], BF16)
    idxg_sb = A("idxg_sb", [128, 2 * GT], I32)
    xn2gt_sb = A("xn2gt_sb", [128, 2 * KT * CAP], BF16)
    wstage_sb = A("wstage_sb", [128, 2 * D], BF16)
    gst_sb = A("gst_sb", [128, 2 * D], BF16)
    carry_sb = A("carry_sb", [1, 2 * D], BF16)
    ht_sb = A("ht_sb", [128, HC * CAP], BF16)
    W1S, W2S = 8, 7
    w1buf = A("w1buf", [128, W1S * KT * 128], BF16)
    w2buf = A("w2buf", [128, W2S * D], BF16)
    zeroi_sb = A("zeroi_sb", [128, (E * CAP + 127) // 128], I32)

    P1 = nc.alloc_psum_tensor("P1", [128, 2048], F32)
    P2 = nc.alloc_psum_tensor("P2", [128, 2048], F32)
    P1K = ["P1a", "P1b", "P1c", "P1d"]
    P2K = ["P2a", "P2b", "P2c", "P2d"]

    S = Sched(nc)

    def mm(out_ap, lhsT, rhs, start, stop, reads, writes):
        S.emit("pe",
               lambda e, o=out_ap, l=lhsT, r=rhs, s=start, t=stop:
               e.matmul(o, lhsT=l, rhs=r, start=s, stop=t),
               reads=reads, writes=writes)

    def tr(out_ap, in_ap, reads, writes, nk=128):
        S.emit("pe",
               lambda e, o=out_ap, i=in_ap, nk=nk:
               e.transpose(o, i, id_sb[0:nk, 0:nk]),
               reads=list(reads) + ["id"], writes=writes)

    def tr_set(P, pkc, pkd, src_ap, src_k, dst_ap, dst_k, eng="dve"):
        for half in range(2):
            pk = pkc if half == 0 else pkd
            for k in range(4 * half, 4 * half + 4):
                tr(P[:, 1024 + k * 128:1024 + (k + 1) * 128],
                   src_ap[:, k * 128:(k + 1) * 128],
                   reads=[src_k], writes=[pk])
            hw = 512
            dview = dst_ap[:, half * hw:(half + 1) * hw]
            if eng == "dve":
                S.emit("dve", lambda e, o=dview, half=half: e.tensor_copy(
                    out=o, in_=P[:, 1024 + half * hw:1024 + (half + 1) * hw]),
                    reads=[pk], writes=[dst_k])
            else:
                S.emit("act", lambda e, o=dview, half=half: e.copy(
                    out=o, in_=P[:, 1024 + half * hw:1024 + (half + 1) * hw]),
                    reads=[pk], writes=[dst_k])

    def cast_load(dst_ap, src_ap, wkey):
        S.emit("qg", lambda e, o=dst_ap, i=src_ap: e.dma_start(out=o, in_=i),
               reads=(), writes=[wkey], dma=True, dsem="ini_g", no_waw=False)

    cast_load(gw_sb[:].rearrange("p (k e) -> p k e", k=KT),
              d_gw.rearrange("(k p) e -> p k e", p=128), "gw")
    cast_load(binrow[:], d_bin[:], "binrow")
    cast_load(boutrow[:], d_bout[:], "boutrow")
    cast_load(gbrow[:], d_gb[:], "gbrow")
    cast_load(b2r[:], d_b2[:], "b2r")
    cast_load(at_sb[:], d_at[:], "at")
    cast_load(dec_sb[:], d_dec[:], "dec")
    cast_load(c128_sb[:], d_c128[:], "c128")
    cast_load(ltu_sb[:], d_ltu[:], "ltu")
    S.emit("qs", lambda e: e.dma_start(out=id_sb[:], in_=d_id[:]),
           reads=(), writes=["id"], dma=True, dsem="ini_s")
    S.emit("qs", lambda e: e.dma_start(out=cexp_sb[:], in_=d_cexp[:]),
           reads=(), writes=["cexp"], dma=True, dsem="ini_s")
    S.emit("qs", lambda e: e.dma_start(out=cids_sb[:], in_=d_cids[:]),
           reads=(), writes=["cids"], dma=True, dsem="ini_s")
    for i in range(NB1):
        S.emit("qs",
               lambda e, i=i: e.dma_start(
                   out=b1r[:, i * 128:(i + 1) * 128],
                   in_=d_b1[i * 128:i * 128 + B1R, :]),
               reads=(), writes=["b1r%d" % i], dma=True, dsem="ini_s")
    S.emit("dve", lambda e: e.memset(ones_sb[:], 1.0), writes=["ones"])
    S.emit("dve", lambda e: e.memset(onesc_sb[:], 1.0), writes=["onesc"])
    S.emit("dve", lambda e: e.memset(eps_c[:], EPS), writes=["eps"])
    S.emit("dve", lambda e: e.memset(nhalf_c[:], -0.5), writes=["nhalf"])
    S.emit("dve", lambda e: e.memset(zeroi_sb[:], 0), writes=["zeroi"])
    S.emit("qg", lambda e: e.dma_start(
        out=d_glist.rearrange("(p a) b -> p (a b)", p=128), in_=zeroi_sb[:]),
        reads=["zeroi"], writes=["glist"], dma=True, dsem="gli")
    for i in range(NB1):
        tr(P2[0:128, 0:B1R], b1r[0:B1R, i * 128:(i + 1) * 128],
           reads=["b1r%d" % i], writes=["P2a"], nk=B1R)
        S.emit("dve",
               lambda e, i=i: e.tensor_copy(
                   out=b1c[:, i * 128:i * 128 + B1R], in_=P2[0:128, 0:B1R]),
               reads=["P2a"], writes=["b1c"])

    fcnt = [0]
    tcnt = [0]

    def f32slot():
        s = fcnt[0] % 8
        fcnt[0] += 1
        return f32p[:, s * D:(s + 1) * D], "f32p%d" % s

    def trslot():
        s = tcnt[0] % 2
        tcnt[0] += 1
        return trp[:, s * D:(s + 1) * D], "trp%d" % s

    def emit_B(c, tt, defer):
        g = c * TPC + tt
        sx = tt % 2
        sxk = "sx%d" % sx
        sxap = sx_sb[:, sx * D:(sx + 1) * D]
        em = S.defer if defer else S.emit
        em("qs", lambda e, g=g, o=sxap: e.dma_start(
            out=o, in_=d_x[g * 128:(g + 1) * 128, :]),
            reads=(), writes=[sxk], dma=True, dsem="sxd%d" % sx)
        em("dve", lambda e, x=sxap: e.bn_stats(
            out=stats_sb[:, 0:6], in_=x[:, 0:512]),
            reads=[sxk], writes=["stats"])
        em("dve", lambda e, x=sxap: e.bn_stats(
            out=stats_sb[:, 6:12], in_=x[:, 512:1024]),
            reads=[sxk], writes=["stats"])
        em("dve", lambda e, tt=tt: e.bn_aggr(
            out=mv_sb[:, 2 * tt:2 * tt + 2], in_=stats_sb[:]),
            reads=["stats"], writes=["mv%d" % tt])
        em("act", lambda e, tt=tt: e.activation(
            out=rstd_sb[:, tt:tt + 1], in_=mv_sb[:, 2 * tt + 1:2 * tt + 2],
            func=AF.Ln, bias=eps_c[:]),
            reads=["mv%d" % tt, "eps"], writes=["rstd%d" % tt])
        em("act", lambda e, tt=tt: e.activation(
            out=rstd_sb[:, tt:tt + 1], in_=rstd_sb[:, tt:tt + 1],
            func=AF.Exp, scale=nhalf_c[:]),
            reads=["rstd%d" % tt, "nhalf"], writes=["rstd%d" % tt])
        em("dve", lambda e, tt=tt: e.scalar_tensor_tensor(
            out=negmr_sb[:, tt:tt + 1], in0=mv_sb[:, 2 * tt:2 * tt + 1],
            scalar=-1.0, in1=rstd_sb[:, tt:tt + 1],
            op0=ALU.mult, op1=ALU.mult),
            reads=["mv%d" % tt, "rstd%d" % tt], writes=["negmr%d" % tt])

    def emit_route(c, tt):
        em = S.emit
        xsl = tt % 2
        xn2t_k = "xn2tt%d" % xsl
        xb = xsl * KT * 128
        mm(P1[0:128, 1024:1024 + E], ones_sb[:], gbrow[:], True, False,
           reads=["ones", "gbrow"], writes=["P1c"])
        for k in range(KT):
            mm(P1[0:128, 1024:1024 + E],
               xn2tt[:, xb + k * 128:xb + (k + 1) * 128],
               gw_sb[:, k * E:(k + 1) * E], False, k == KT - 1,
               reads=[xn2t_k, "gw"], writes=["P1c"])
        em("dve", lambda e: e.tensor_copy(
            out=g_sb[:], in_=P1[0:128, 1024:1024 + E]),
            reads=["P1c"], writes=["g"])
        em("dve", lambda e: e.reduce_max(
            out=s1_sb[:], in_=g_sb[:], axis=AX.X), reads=["g"], writes=["s1"])
        em("dve", lambda e: e.tensor_scalar(
            out=gsh_sb[:], in0=g_sb[:], scalar1=s1_sb[:], scalar2=None,
            op0=ALU.subtract), reads=["g", "s1"], writes=["gsh"])
        em("act", lambda e: e.activation(
            out=ez_sb[:], in_=gsh_sb[:], func=AF.Exp),
            reads=["gsh"], writes=["ez"])
        em("dve", lambda e: e.reduce_sum(
            out=s2_sb[:], in_=ez_sb[:], axis=AX.X), reads=["ez"], writes=["s2"])
        em("dve", lambda e: e.reciprocal(
            out=s2_sb[:], in_=s2_sb[:]), reads=["s2"], writes=["s2"])
        wap = m_sb[:, tt * E:(tt + 1) * E]
        wk = "m%d" % tt
        em("dve", lambda e, o=wap: e.tensor_scalar(
            out=o, in0=ez_sb[:], scalar1=s2_sb[:], scalar2=None,
            op0=ALU.mult), reads=["ez", "s2"], writes=[wk])
        em("dve", lambda e, w=wap: e.reduce_max(
            out=s1_sb[:], in_=w, axis=AX.X), reads=[wk], writes=["s1"])
        em("dve", lambda e, w=wap: e.tensor_scalar(
            out=sel1_sb[:], in0=w, scalar1=s1_sb[:], scalar2=None,
            op0=ALU.is_ge), reads=[wk, "s1"], writes=["sel1"])
        em("dve", lambda e, w=wap: e.scalar_tensor_tensor(
            out=wm_sb[:], in0=sel1_sb[:], scalar=-1e9, in1=w,
            op0=ALU.mult, op1=ALU.add),
            reads=["sel1", wk], writes=["wm"])
        em("dve", lambda e: e.reduce_max(
            out=s3_sb[:], in_=wm_sb[:], axis=AX.X), reads=["wm"], writes=["s3"])
        em("dve", lambda e, w=wap: e.tensor_scalar(
            out=sel_sb[:], in0=w, scalar1=s3_sb[:], scalar2=None,
            op0=ALU.is_ge), reads=[wk, "s3"], writes=["sel"])
        em("dve", lambda e: e.tensor_copy(
            out=selu_sb[:], in_=sel_sb[:]), reads=["sel"], writes=["selu"])
        em("dve", lambda e, w=wap, o=wap: e.tensor_tensor(
            out=o, in0=w, in1=sel_sb[:], op=ALU.mult),
            reads=[wk, "sel"], writes=[wk])
        em("dve", lambda e, w=wap, tt=tt: e.reduce_max(
            out=m12_sb[:, 2 * tt:2 * tt + 1], in_=w, axis=AX.X),
            reads=[wk], writes=["m12_%d" % tt])
        em("dve", lambda e, w=wap: e.reduce_sum(
            out=s2_sb[:], in_=w, axis=AX.X), reads=[wk], writes=["s2"])
        em("dve", lambda e, tt=tt: e.scalar_tensor_tensor(
            out=m12_sb[:, 2 * tt + 1:2 * tt + 2],
            in0=m12_sb[:, 2 * tt:2 * tt + 1], scalar=-1.0, in1=s2_sb[:],
            op0=ALU.mult, op1=ALU.add),
            reads=["m12_%d" % tt, "s2"], writes=["m12_%d" % tt])
        em("dve", lambda e: e.tensor_tensor(
            out=ismax_sb[:], in0=sel1_sb[:], in1=cexp_sb[:], op=ALU.mult),
            reads=["sel1", "cexp"], writes=["ismax"])
        em("dve", lambda e, tt=tt: e.reduce_sum(
            out=e12_sb[:, 2 * tt:2 * tt + 1], in_=ismax_sb[:], axis=AX.X),
            reads=["ismax"], writes=["e12_%d" % tt])
        em("dve", lambda e: e.tensor_tensor(
            out=ismax_sb[:], in0=selu_sb[:], in1=cexp_sb[:], op=ALU.mult),
            reads=["selu", "cexp"], writes=["ismax"])
        em("dve", lambda e: e.reduce_sum(
            out=s2_sb[:], in_=ismax_sb[:], axis=AX.X),
            reads=["ismax"], writes=["s2"])
        em("dve", lambda e, tt=tt: e.scalar_tensor_tensor(
            out=e12_sb[:, 2 * tt + 1:2 * tt + 2],
            in0=e12_sb[:, 2 * tt:2 * tt + 1], scalar=-1.0, in1=s2_sb[:],
            op0=ALU.mult, op1=ALU.add),
            reads=["e12_%d" % tt, "s2"], writes=["e12_%d" % tt])
        sbap = selub_sb[:, tt * E:(tt + 1) * E]
        sbk = "selub%d" % tt
        em("dve", lambda e, o=sbap: e.tensor_copy(out=o, in_=selu_sb[:]),
           reads=["selu"], writes=[sbk])
        em("dve", lambda e, tt=tt: e.tensor_copy(
            out=sel1t_sb[:, tt * E:(tt + 1) * E], in_=sel1_sb[:]),
            reads=["sel1"], writes=["sel1t%d" % tt])

    def emit_route_b(c, tt):
        em = S.emit
        sbap = selub_sb[:, tt * E:(tt + 1) * E]
        sbk = "selub%d" % tt
        s1ap = sel1t_sb[:, tt * E:(tt + 1) * E]
        s1k = "sel1t%d" % tt
        S.emit("pe", lambda e, i=sbap: e.matmul(
            P1[0:1, 1536:1536 + E], lhsT=onesc_sb[:], rhs=i,
            start=True, stop=True),
            reads=["onesc", sbk], writes=["P1d"])
        em("dve", lambda e, tt=tt: e.tensor_copy(
            out=cnt_sb[0:1, tt * E:(tt + 1) * E], in_=P1[0:1, 1536:1536 + E]),
            reads=["P1d"], writes=["cnt%d" % tt])
        S.emit("pe", lambda e, i=sbap: e.matmul(
            P1[0:128, 1040:1040 + E], lhsT=ltu_sb[:], rhs=i,
            start=True, stop=(tt == 0)),
            reads=["ltu", sbk], writes=["P1c"])
        for jp in range(tt):
            S.emit("pe", lambda e, jp=jp: e.matmul(
                P1[0:128, 1040:1040 + E], lhsT=ones_sb[:],
                rhs=cnt_sb[0:1, jp * E:(jp + 1) * E],
                start=False, stop=(jp == tt - 1)),
                reads=["ones", "cnt%d" % jp], writes=["P1c"])
        em("dve", lambda e: e.tensor_copy(
            out=rank_sb[:], in_=P1[0:128, 1040:1040 + E]),
            reads=["P1c"], writes=["rank"])
        em("dve", lambda e, a=s1ap: e.tensor_tensor(
            out=ismax_sb[:], in0=a, in1=rank_sb[:], op=ALU.mult),
            reads=[s1k, "rank"], writes=["ismax"])
        em("dve", lambda e, tt=tt: e.reduce_sum(
            out=r12_sb[:, 2 * tt:2 * tt + 1], in_=ismax_sb[:], axis=AX.X),
            reads=["ismax"], writes=["r12_%d" % tt])
        em("dve", lambda e, a=sbap: e.tensor_tensor(
            out=ismax_sb[:], in0=a, in1=rank_sb[:], op=ALU.mult),
            reads=[sbk, "rank"], writes=["ismax"])
        em("dve", lambda e: e.reduce_sum(
            out=s2_sb[:], in_=ismax_sb[:], axis=AX.X),
            reads=["ismax"], writes=["s2"])
        em("dve", lambda e, tt=tt: e.scalar_tensor_tensor(
            out=r12_sb[:, 2 * tt + 1:2 * tt + 2],
            in0=r12_sb[:, 2 * tt:2 * tt + 1], scalar=-1.0, in1=s2_sb[:],
            op0=ALU.mult, op1=ALU.add),
            reads=["r12_%d" % tt, "s2"], writes=["r12_%d" % tt])
        em("dve", lambda e, tt=tt: e.tensor_scalar(
            out=r12_sb[:, 2 * tt:2 * tt + 2],
            in0=r12_sb[:, 2 * tt:2 * tt + 2],
            scalar1=float(CAP - 1), scalar2=None, op0=ALU.min),
            reads=["r12_%d" % tt], writes=["r12_%d" % tt])
        for s in range(2):
            em("dve", lambda e, tt=tt, s=s: e.scalar_tensor_tensor(
                out=idxf_sb[:, 2 * tt + s:2 * tt + s + 1],
                in0=e12_sb[:, 2 * tt + s:2 * tt + s + 1],
                scalar=float(CAP),
                in1=r12_sb[:, 2 * tt + s:2 * tt + s + 1],
                op0=ALU.mult, op1=ALU.add),
                reads=["e12_%d" % tt, "r12_%d" % tt],
                writes=["idxf_%d" % tt])
        em("dve", lambda e, tt=tt: e.tensor_copy(
            out=idxi_sb[:, 2 * tt:2 * tt + 2],
            in_=idxf_sb[:, 2 * tt:2 * tt + 2]),
            reads=["idxf_%d" % tt], writes=["idxi%d" % tt])
        for s in range(2):
            S.emit("qg", lambda e, tt=tt, s=s: e.indirect_dma_start(
                out=d_glist[:],
                out_offset=IndirectOffsetOnAxis(
                    ap=idxi_sb[:, 2 * tt + s:2 * tt + s + 1], axis=0),
                in_=cids_sb[:, tt:tt + 1], in_offset=None),
                reads=["idxi%d" % tt, "cids"], writes=["glist"],
                dma=True, dsem="gsc", no_waw=True)

    attn_state = {}

    def emit_attn_s1(c, tt):
        g = c * TPC + tt
        xk = "xa%d" % (tt % 4)
        xap = xa_sb[:, (tt % 4) * D:((tt % 4) + 1) * D]
        S.emit("qs", lambda e, g=g, o=xap: e.dma_start(
            out=o, in_=d_x[g * 128:(g + 1) * 128, :]),
            reads=(), writes=[xk], dma=True, dsem="xad%d" % (tt % 4))
        xn_ap, xn_k = f32slot()
        S.emit("act", lambda e, x=xap, o=xn_ap, tt=tt: e.activation(
            out=o, in_=x, func=AF.Identity,
            bias=negmr_sb[:, tt:tt + 1], scale=rstd_sb[:, tt:tt + 1]),
            reads=[xk, "negmr%d" % tt, "rstd%d" % tt], writes=[xn_k])
        xnt_ap, xnt_k = trslot()
        tr_set(P2, "P2c", "P2d", xn_ap, xn_k, xnt_ap, xnt_k)
        attn_state[tt] = (xap, xk, xnt_ap, xnt_k)

    def emit_attn_s2pair(c, p):
        first_chunk = (c == 0)
        tiles = [p, p + 1]
        P_of = {p: (P1, P1K), p + 1: (P2, P2K)}
        for tt in tiles:
            P, PK = P_of[tt]
            for nch in range(4):
                mm(P[:, nch * 512:(nch + 1) * 512], ones_sb[:],
                   binrow[:, nch * 512:(nch + 1) * 512], True, False,
                   reads=["ones", "binrow"], writes=[PK[nch]])
        for k in range(KT):
            wsl = k % 2
            wks = "winb%d" % wsl
            wap_ = winbuf[:, wsl * 2 * D:(wsl + 1) * 2 * D]
            if first_chunk and p == 0:
                S.emit("qg", lambda e, k=k, o=wap_: e.dma_start(
                    out=o, in_=d_win[k * 128:(k + 1) * 128, :]),
                    reads=(), writes=[wks], dma=True, dsem="wind%d" % wsl)
                S.emit("qs", lambda e, k=k, i=wap_: e.dma_start(
                    out=d_swin[k, :, :], in_=i),
                    reads=[wks], writes=["swin%d" % k], dma=True,
                    dsem="wins%d" % wsl)
            else:
                S.emit("qs", lambda e, k=k, o=wap_: e.dma_start(
                    out=o, in_=d_swin[k, :, :]),
                    reads=["swin%d" % k], writes=[wks], dma=True,
                    dsem="winq%d" % wsl)
            for tt in tiles:
                P, PK = P_of[tt]
                xnt_ap, xnt_k = attn_state[tt][2], attn_state[tt][3]
                for nch in range(4):
                    mm(P[:, nch * 512:(nch + 1) * 512],
                       xnt_ap[:, k * 128:(k + 1) * 128],
                       wap_[:, nch * 512:(nch + 1) * 512],
                       False, k == KT - 1,
                       reads=[xnt_k, wks], writes=[PK[nch]])
        for tt in tiles:
            P, PK = P_of[tt]
            S.emit("act", lambda e, P=P: e.activation(
                out=sg_sb[:], in_=P[:, 0:1024], func=AF.Sigmoid),
                reads=[PK[0], PK[1]], writes=["sg"])
            S.emit("act", lambda e, P=P: e.activation(
                out=sh_sb[:], in_=P[:, 1024:2048], func=AF.Sigmoid),
                reads=[PK[2], PK[3]], writes=["sh"])
            S.emit("dve", lambda e, P=P: e.tensor_tensor(
                out=sh_sb[:], in0=sh_sb[:], in1=P[:, 1024:2048], op=ALU.mult),
                reads=["sh", PK[2], PK[3]], writes=["sh"])
            xg_ap = xg_sb[:, (tt % 2) * D:((tt % 2) + 1) * D]
            xg_k = "xg%d" % (tt % 2)
            S.emit("dve", lambda e, o=xg_ap: e.tensor_tensor(
                out=o, in0=sg_sb[:], in1=sh_sb[:], op=ALU.mult),
                reads=["sg", "sh"], writes=[xg_k])

    def emit_attn_s3(c, tt):
        g = c * TPC + tt
        xg_ap = xg_sb[:, (tt % 2) * D:((tt % 2) + 1) * D]
        xg_k = "xg%d" % (tt % 2)
        cprev = carry_sb[0:1, ((g + 1) % 2) * D:((g + 1) % 2 + 1) * D]
        cprev_k = "cr%d" % ((g + 1) % 2)
        ccur = carry_sb[0:1, (g % 2) * D:(g % 2 + 1) * D]
        ccur_k = "cr%d" % (g % 2)
        for nch in range(2):
            mm(P2[:, nch * 512:(nch + 1) * 512], at_sb[:],
               xg_ap[:, nch * 512:(nch + 1) * 512], True, g == 0,
               reads=["at", xg_k], writes=[P2K[nch]])
            if g > 0:
                mm(P2[:, nch * 512:(nch + 1) * 512], dec_sb[:],
                   cprev[:, nch * 512:(nch + 1) * 512], False, True,
                   reads=["dec", cprev_k], writes=[P2K[nch]])
        for nch in range(2):
            mm(P2[0:1, 1024 + nch * 512:1024 + (nch + 1) * 512],
               at_sb[:, 127:128], xg_ap[:, nch * 512:(nch + 1) * 512],
               True, g == 0,
               reads=["at", xg_k], writes=[P2K[2 + nch]])
            if g > 0:
                mm(P2[0:1, 1024 + nch * 512:1024 + (nch + 1) * 512],
                   c128_sb[:], cprev[:, nch * 512:(nch + 1) * 512],
                   False, True,
                   reads=["c128", cprev_k], writes=[P2K[2 + nch]])
        S.emit("act", lambda e, o=ccur: e.activation(
            out=o, in_=P2[0:1, 1024:2048], func=AF.Copy),
            reads=["P2c", "P2d"], writes=[ccur_k])
        hid_ap, hid_k = f32slot()
        S.emit("dve", lambda e, o=hid_ap: e.tensor_copy(
            out=o, in_=P2[:, 0:1024]),
            reads=["P2a", "P2b"], writes=[hid_k])
        ht_ap, ht_k = trslot()
        tr_set(P1, "P1c", "P1d", hid_ap, hid_k, ht_ap, ht_k)
        attn_state[(tt, "ht")] = (ht_ap, ht_k)

    def emit_attn_s4a_pair(c, p):
        first_chunk = (c == 0)
        tiles = [p, p + 1]
        P_of = {p: (P1, P1K), p + 1: (P2, P2K)}
        for tt in tiles:
            P, PK = P_of[tt]
            for nch in range(2):
                mm(P[:, nch * 512:(nch + 1) * 512], ones_sb[:],
                   boutrow[:, nch * 512:(nch + 1) * 512], True, False,
                   reads=["ones", "boutrow"], writes=[PK[nch]])
        for k in range(KT):
            wsl = k % 2
            wks = "woutb%d" % wsl
            wap_ = woutbuf[:, wsl * D:(wsl + 1) * D]
            if first_chunk and p == 0:
                S.emit("qg", lambda e, k=k, o=wap_: e.dma_start(
                    out=o, in_=d_wout[k * 128:(k + 1) * 128, :]),
                    reads=(), writes=[wks], dma=True, dsem="wod%d" % wsl)
                S.emit("qs", lambda e, k=k, i=wap_: e.dma_start(
                    out=d_swout[k, :, :], in_=i),
                    reads=[wks], writes=["swout%d" % k], dma=True,
                    dsem="wos%d" % wsl)
            else:
                S.emit("qs", lambda e, k=k, o=wap_: e.dma_start(
                    out=o, in_=d_swout[k, :, :]),
                    reads=["swout%d" % k], writes=[wks], dma=True,
                    dsem="woq%d" % wsl)
            for tt in tiles:
                P, PK = P_of[tt]
                ht_ap, ht_k = attn_state[(tt, "ht")]
                for nch in range(2):
                    mm(P[:, nch * 512:(nch + 1) * 512],
                       ht_ap[:, k * 128:(k + 1) * 128],
                       wap_[:, nch * 512:(nch + 1) * 512],
                       False, k == KT - 1,
                       reads=[ht_k, wks], writes=[PK[nch]])
        for tt in tiles:
            P, PK = P_of[tt]
            xap, xk = attn_state[tt][0], attn_state[tt][1]
            x1ap = x1r_sb[:, (tt % 2) * D:((tt % 2) + 1) * D]
            x1k = "x1r%d" % (tt % 2)
            S.emit("dve", lambda e, x=xap, o=x1ap, P=P: e.tensor_tensor(
                out=o, in0=P[:, 0:1024], in1=x, op=ALU.add),
                reads=[PK[0], PK[1], xk], writes=[x1k])

    def emit_attn_s4b(c, tt):
        g = c * TPC + tt
        x1ap = x1r_sb[:, (tt % 2) * D:((tt % 2) + 1) * D]
        x1k = "x1r%d" % (tt % 2)
        S.emit("dve", lambda e, i=x1ap: e.bn_stats(
            out=stats_sb[:, 0:6], in_=i[:, 0:512]),
            reads=[x1k], writes=["stats"])
        S.emit("dve", lambda e, i=x1ap: e.bn_stats(
            out=stats_sb[:, 6:12], in_=i[:, 512:1024]),
            reads=[x1k], writes=["stats"])
        S.emit("dve", lambda e, tt=tt: e.bn_aggr(
            out=mv_sb[:, 2 * TPC + 2 * tt:2 * TPC + 2 * tt + 2],
            in_=stats_sb[:]),
            reads=["stats"], writes=["mv2_%d" % tt])
        S.emit("act", lambda e, tt=tt: e.activation(
            out=rstd_sb[:, TPC + tt:TPC + tt + 1],
            in_=mv_sb[:, 2 * TPC + 2 * tt + 1:2 * TPC + 2 * tt + 2],
            func=AF.Ln, bias=eps_c[:]),
            reads=["mv2_%d" % tt, "eps"], writes=["rstd2_%d" % tt])
        S.emit("act", lambda e, tt=tt: e.activation(
            out=rstd_sb[:, TPC + tt:TPC + tt + 1],
            in_=rstd_sb[:, TPC + tt:TPC + tt + 1],
            func=AF.Exp, scale=nhalf_c[:]),
            reads=["rstd2_%d" % tt, "nhalf"], writes=["rstd2_%d" % tt])
        S.emit("dve", lambda e, tt=tt: e.scalar_tensor_tensor(
            out=negmr_sb[:, TPC + tt:TPC + tt + 1],
            in0=mv_sb[:, 2 * TPC + 2 * tt:2 * TPC + 2 * tt + 1],
            scalar=-1.0, in1=rstd_sb[:, TPC + tt:TPC + tt + 1],
            op0=ALU.mult, op1=ALU.mult),
            reads=["mv2_%d" % tt, "rstd2_%d" % tt], writes=["negmr2_%d" % tt])
        xn2_ap, xn2_k = f32slot()
        S.emit("act", lambda e, o=xn2_ap, i=x1ap, tt=tt: e.activation(
            out=o, in_=i, func=AF.Identity,
            bias=negmr_sb[:, TPC + tt:TPC + tt + 1],
            scale=rstd_sb[:, TPC + tt:TPC + tt + 1]),
            reads=[x1k, "negmr2_%d" % tt, "rstd2_%d" % tt],
            writes=[xn2_k])
        S.emit("act", lambda e, i=x1ap, tt=tt: e.dma_start(
            out=d_x1s[tt * 128:(tt + 1) * 128, :], in_=i),
            reads=[x1k], writes=["x1s%d" % tt], dma=True,
            dsem="x1sd%d" % (tt % 2))
        xsl = tt % 2
        tr_set(P1, "P1c", "P1d", xn2_ap,
               xn2_k, xn2tt[:, xsl * KT * 128:(xsl + 1) * KT * 128],
               "xn2tt%d" % xsl)
        S.emit("qg", lambda e, i=xn2_ap, tt=tt: e.dma_start(
            out=d_xn2s[tt * 128:(tt + 1) * 128, :], in_=i),
            reads=[xn2_k], writes=["xn2s%d" % tt], dma=True,
            dsem="xsp%d" % (tt % 2))

    gslots = {}

    def emit_gather_dma(c, ex):
        xn2s_keys = ["xn2s%d" % t for t in range(TPC)]
        sl = ex % 2
        slots = []
        for gt in range(GT):
            ig = GT * sl + gt
            S.emit("qg", lambda e, ex=ex, gt=gt, ig=ig: e.dma_start(
                out=idxg_sb[:, ig:ig + 1],
                in_=d_glist[ex * CAP + gt * 128:ex * CAP + (gt + 1) * 128, :]),
                reads=["glist"], writes=["idxgs%d" % sl], dma=True,
                dsem="idg", no_waw=True)
            gap_ap, gap_k = f32slot()
            S.emit("qg", lambda e, ig=ig, o=gap_ap: e.indirect_dma_start(
                out=o, out_offset=None,
                in_=d_xn2s[:],
                in_offset=IndirectOffsetOnAxis(
                    ap=idxg_sb[:, ig:ig + 1], axis=0)),
                reads=xn2s_keys + ["idxgs%d" % sl],
                writes=[gap_k], dma=True, dsem="xng_" + gap_k)
            slots.append((gap_ap, gap_k))
        gslots[ex] = slots

    def emit_gather_tr(c, ex):
        sl = ex % 2
        base = sl * KT * CAP
        for gt in range(GT):
            gap_ap, gap_k = gslots[ex][gt]
            for half in range(2):
                pk = "P2c" if half == 0 else "P2d"
                for k in range(4 * half, 4 * half + 4):
                    tr(P2[:, 1024 + k * 128:1024 + (k + 1) * 128],
                       gap_ap[:, k * 128:(k + 1) * 128],
                       reads=[gap_k], writes=[pk])
                xg_dst = (xn2gt_sb[:, base:base + KT * CAP]
                          .rearrange("p (k t) -> p k t", t=CAP)
                          [:, 4 * half:4 * half + 4, gt * 128:(gt + 1) * 128])
                pview = (P2[:, 1024 + half * 512:1536 + half * 512]
                         .rearrange("p (k t) -> p k t", t=128))
                if (gt + half) % 2 == 0:
                    S.emit("act", lambda e, o=xg_dst, i=pview: e.copy(
                        out=o, in_=i),
                        reads=[pk], writes=["xn2gt%d" % sl])
                else:
                    S.emit("dve", lambda e, o=xg_dst, i=pview: e.tensor_copy(
                        out=o, in_=i),
                        reads=[pk], writes=["xn2gt%d" % sl])

    def emit_moe(c):
        first_chunk = (c == 0)
        emit_gather_dma(c, 0)
        if E > 1:
            emit_gather_dma(c, 1)
        emit_gather_tr(c, 0)
        for ex in range(E):
            if c + 1 < NCH and ex < TPC:
                emit_B(c + 1, ex, defer=True)
            sl = ex % 2
            base = sl * KT * CAP
            if ex + 2 < E:
                emit_gather_dma(c, ex + 2)
            if ex + 1 < E:
                emit_gather_tr(c, ex + 1)
            for hc in range(HC):
                wi = (ex * HC + hc) % W1S
                wk = "w1b%d" % wi
                sk = "s1_%d_%d" % (ex, hc)
                if first_chunk:
                    S.emit("qg",
                           lambda e, ex=ex, hc=hc, wi=wi: e.dma_start(
                               out=w1buf[:, wi * KT * 128:(wi + 1) * KT * 128]
                               .rearrange("p (k q) -> p k q", q=128),
                               in_=d_w1[ex, :, hc * 128:(hc + 1) * 128]
                               .rearrange("(k p) q -> p k q", p=128)),
                           reads=(), writes=[wk], dma=True)
                    S.emit("qs",
                           lambda e, ex=ex, hc=hc, wi=wi: e.dma_start(
                               out=d_s1[ex, hc, :, :],
                               in_=w1buf[:, wi * KT * 128:(wi + 1) * KT * 128]),
                           reads=[wk], writes=[sk], dma=True,
                           dsem="sw1_%d" % wi)
                else:
                    S.emit("qs",
                           lambda e, ex=ex, hc=hc, wi=wi: e.dma_start(
                               out=w1buf[:, wi * KT * 128:(wi + 1) * KT * 128],
                               in_=d_s1[ex, hc, :, :]),
                           reads=[sk], writes=[wk], dma=True,
                           dsem="w1q%d" % wi)
                ps = P1K[hc % 4]
                pof = 512 * (hc % 4)
                for k in range(KT):
                    mm(P1[:, pof:pof + 512],
                       w1buf[:, wi * KT * 128 + k * 128:wi * KT * 128 + (k + 1) * 128],
                       xn2gt_sb[:, base + k * CAP:base + (k + 1) * CAP],
                       k == 0, k == KT - 1,
                       reads=[wk, "xn2gt%d" % sl], writes=[ps])
                S.pump(2)
                hk = "ht%d" % hc
                if hc % 2 == 0:
                    S.emit("act", lambda e, ex=ex, hc=hc, pof=pof: e.activation(
                        out=ht_sb[:, hc * CAP:(hc + 1) * CAP],
                        in_=P1[:, pof:pof + CAP], func=AF.Relu,
                        bias=b1c[:, ex * HC + hc:ex * HC + hc + 1]),
                        reads=[ps, "b1c"], writes=[hk])
                else:
                    S.emit("dve", lambda e, ex=ex, hc=hc, pof=pof: e.tensor_scalar(
                        out=ht_sb[:, hc * CAP:(hc + 1) * CAP],
                        in0=P1[:, pof:pof + CAP],
                        scalar1=b1c[:, ex * HC + hc:ex * HC + hc + 1],
                        scalar2=0.0, op0=ALU.add, op1=ALU.max),
                        reads=[ps, "b1c"], writes=[hk])
            w2slots = [("P1", 0), ("P1", 1024), ("P2", 0), ("P2", 1024)]
            for kt in range(HC):
                wi = kt % W2S
                wk = "w2b%d" % wi
                sk = "s2_%d_%d" % (ex, kt)
                if first_chunk:
                    S.emit("qg",
                           lambda e, ex=ex, kt=kt, wi=wi: e.dma_start(
                               out=w2buf[:, wi * D:(wi + 1) * D],
                               in_=d_w2[ex, kt * 128:(kt + 1) * 128, :]),
                           reads=(), writes=[wk], dma=True)
                    S.emit("qs",
                           lambda e, ex=ex, kt=kt, wi=wi: e.dma_start(
                               out=d_s2[ex, kt, :, :],
                               in_=w2buf[:, wi * D:(wi + 1) * D]),
                           reads=[wk], writes=[sk], dma=True,
                           dsem="sw2_%d" % wi)
                else:
                    S.emit("qs",
                           lambda e, ex=ex, kt=kt, wi=wi: e.dma_start(
                               out=w2buf[:, wi * D:(wi + 1) * D],
                               in_=d_s2[ex, kt, :, :]),
                           reads=[sk], writes=[wk], dma=True,
                           dsem="w2q%d" % wi)
                for rt in range(GT):
                    pn, po = w2slots[rt]
                    P = P1 if pn == "P1" else P2
                    pk = (P1K if pn == "P1" else P2K)
                    for nch in range(2):
                        mm(P[:, po + nch * 512:po + (nch + 1) * 512],
                           ht_sb[:, kt * CAP + rt * 128:kt * CAP + (rt + 1) * 128],
                           w2buf[:, wi * D + nch * 512:wi * D + (nch + 1) * 512],
                           kt == 0, kt == HC - 1,
                           reads=["ht%d" % kt, wk],
                           writes=[pk[po // 512 + nch]])

            for rt in range(GT):
                pn, po = w2slots[rt]
                P = P1 if pn == "P1" else P2
                pk = (P1K if pn == "P1" else P2K)
                ws = rt % 2
                S.emit("act", lambda e, P=P, po=po, ws=ws: e.copy(
                    out=wstage_sb[:, ws * D:(ws + 1) * D],
                    in_=P[:, po:po + 1024]),
                    reads=[pk[po // 512], pk[po // 512 + 1]],
                    writes=["wst%d" % ws])
                S.emit("act", lambda e, ex=ex, rt=rt, ws=ws: e.dma_start(
                    out=d_stage[ex * CAP + rt * 128:ex * CAP + (rt + 1) * 128, :],
                    in_=wstage_sb[:, ws * D:(ws + 1) * D]),
                    reads=["wst%d" % ws], writes=["stage"], dma=True,
                    dsem="stw%d" % ws)

    def emit_combine_pre(c):
        for tt in range(TPC):
            tr(P2[0:E, 1024 + (tt % 2) * 512:1152 + (tt % 2) * 512],
               m_sb[:, tt * E:(tt + 1) * E],
               reads=["m%d" % tt], writes=["P2c" if tt % 2 == 0 else "P2d"])
            S.emit("dve", lambda e, tt=tt: e.tensor_copy(
                out=mt_sb[0:E, tt * 128:(tt + 1) * 128],
                in_=P2[0:E, 1024 + (tt % 2) * 512:1152 + (tt % 2) * 512]),
                reads=["P2c" if tt % 2 == 0 else "P2d"], writes=["mt%d" % tt])

    def emit_combine(c):
        for tt in range(TPC):
            g = c * TPC + tt
            x1ap = x1r_sb[:, (tt % 2) * D:((tt % 2) + 1) * D]
            x1k = "x1r%d" % (tt % 2)
            S.emit("qs", lambda e, tt=tt, o=x1ap: e.dma_start(
                out=o, in_=d_x1s[tt * 128:(tt + 1) * 128, :]),
                reads=["x1s%d" % tt], writes=[x1k], dma=True,
                dsem="x1ld%d" % (tt % 2))
            for s in range(2):
                S.emit("qg", lambda e, tt=tt, s=s: e.indirect_dma_start(
                    out=gst_sb[:, s * D:(s + 1) * D], out_offset=None,
                    in_=d_stage[:],
                    in_offset=IndirectOffsetOnAxis(
                        ap=idxi_sb[:, 2 * tt + s:2 * tt + s + 1], axis=0)),
                    reads=["stage", "idxi%d" % tt], writes=["gst%d" % s],
                    dma=True, dsem="gso%d" % s)
            bo = 1024 * (tt % 2)
            for nch in range(2):
                mm(P2[:, bo + nch * 512:bo + (nch + 1) * 512],
                   mt_sb[0:E, tt * 128:(tt + 1) * 128],
                   b2r[0:E, nch * 512:(nch + 1) * 512], True, True,
                   reads=["mt%d" % tt, "b2r"], writes=[P2K[2 * (tt % 2) + nch]])
            S.emit("dve", lambda e, o=x1ap, bo=bo: e.tensor_tensor(
                out=o, in0=P2[:, bo:bo + 1024], in1=o, op=ALU.add),
                reads=[P2K[2 * (tt % 2)], P2K[2 * (tt % 2) + 1], x1k],
                writes=[x1k])
            for s in range(2):
                S.emit("dve",
                       lambda e, tt=tt, s=s, o=x1ap: e.scalar_tensor_tensor(
                           out=o, in0=gst_sb[:, s * D:(s + 1) * D],
                           scalar=m12_sb[:, 2 * tt + s:2 * tt + s + 1],
                           in1=o, op0=ALU.mult, op1=ALU.add),
                       reads=["gst%d" % s, "m12_%d" % tt, x1k],
                       writes=[x1k])
            S.emit("qs", lambda e, g=g, i=x1ap: e.dma_start(
                out=d_out[g * 128:(g + 1) * 128, :], in_=i),
                reads=[x1k], writes=["outd%d" % g], dma=True,
                dsem="o%d" % (tt % 2))

    for c in range(NCH):
        if c == 0:
            for tt in range(TPC):
                emit_B(0, tt, defer=False)
        for p in range(0, TPC, 2):
            emit_attn_s1(c, p)
            emit_attn_s1(c, p + 1)
            emit_attn_s2pair(c, p)
            emit_attn_s3(c, p)
            emit_attn_s3(c, p + 1)
            emit_attn_s4a_pair(c, p)
            emit_attn_s4b(c, p)
            emit_attn_s4b(c, p + 1)
            emit_route(c, p)
            emit_route(c, p + 1)
            if p >= 4:
                emit_route_b(c, p - 4)
                emit_route_b(c, p - 3)
        for tt in range(TPC - 4, TPC):
            emit_route_b(c, tt)
        emit_combine_pre(c)
        emit_moe(c)
        S.drain()
        emit_combine(c)

    S.finalize_tokens()
    out_totals = [(S.sems["d_o%d" % p], S.totals["d_o%d" % p])
                  for p in range(2)]

    def _final(e, pairs=out_totals):
        for sem, v in pairs:
            e.wait_ge(sem, v)
        return e.nop()

    fin = _Op(_final, "qs", "qs", False)
    S.ops["qs"].append(fin)

    with nc.Block() as blk:
        @blk.tensor
        def _(e):
            S.replay("pe", e)

        @blk.vector
        def _(e):
            S.replay("dve", e)

        @blk.scalar
        def _(e):
            S.replay("act", e)

        @blk.sync
        def _(e):
            S.replay("qs", e)

        @blk.gpsimd
        def _(e):
            S.replay("qg", e)

    return nc


def make_consts(TPC):
    t = np.arange(128)
    AT = np.where(t[None, :] >= t[:, None],
                  INPUT_SCALE * DECAY ** (t[None, :] - t[:, None]), 0.0
                  ).astype(np.float32)
    dec = (DECAY ** (t + 1)).astype(np.float32)[None, :]
    c128 = np.array([[DECAY ** 128]], dtype=np.float32)
    ident = np.eye(128, dtype=np.float32)
    ltu = np.triu(np.ones((128, 128), dtype=np.float32), k=1)
    cexp = np.tile(np.arange(E, dtype=np.float32), (128, 1))
    cids = (np.arange(TPC, dtype=np.int32)[None, :] * 128
            + np.arange(128, dtype=np.int32)[:, None]).astype(np.int32)
    return AT, dec, c128, ident, ltu, cexp, cids


def make_in_maps(x, Win, b_in, Wout, b_out, gw, gb, w1, b1, w2, b2,
                 SEQ=4096, H=4096, CHUNK=1024):
    HC = H // 128
    AT, dec, c128, ident, ltu, cexp, cids = make_consts(CHUNK // 128)
    f32 = lambda a: np.ascontiguousarray(np.asarray(a, dtype=np.float32))
    common = dict(
        Win=f32(Win), bin=f32(b_in).reshape(1, 2 * D),
        Wout=f32(Wout), bout=f32(b_out).reshape(1, D),
        gw=f32(gw), gb=f32(gb).reshape(1, E),
        w1=f32(w1), b1=f32(b1).reshape(E * HC, 128),
        w2=f32(w2), b2=f32(b2),
        cAT=AT, cdec=dec, c128=c128, cid=ident,
        cltu=ltu, cexp=cexp, cids=cids,
    )
    xs = f32(x)
    return [dict(common, x=np.ascontiguousarray(xs[b]))
            for b in range(xs.shape[0])]


def kernel(L0_attn_x, L0_attn_W_in, L0_attn_b_in, L0_attn_W_out, L0_attn_b_out,
           L0_ffn_gate_w, L0_ffn_gate_b, L0_ffn_w1, L0_ffn_b1, L0_ffn_w2,
           L0_ffn_b2):
    nc = build_program(SEQ=4096, CHUNK=1024, H=4096)
    in_maps = make_in_maps(L0_attn_x, L0_attn_W_in, L0_attn_b_in,
                           L0_attn_W_out, L0_attn_b_out, L0_ffn_gate_w,
                           L0_ffn_gate_b, L0_ffn_w1, L0_ffn_b1, L0_ffn_w2,
                           L0_ffn_b2)
    res = run_bass_kernel_spmd(nc, in_maps, core_ids=list(range(N_CORES)))
    out = np.stack([np.asarray(res.results[b]["out"]) for b in range(N_CORES)],
                   axis=0)
    return out.astype(np.float32)
